# version 01
# baseline (speedup 1.0000x reference)
from contextlib import ExitStack

import numpy as np

import concourse.bass as bass
import concourse.mybir as mybir

F32 = mybir.dt.float32
BF16 = mybir.dt.bfloat16
AF = mybir.ActivationFunctionType
ALU = mybir.AluOpType
AX = mybir.AxisListType

D = 4096
NCORES = 8
TLOC = 1024
NTB = TLOC // 128
LN_EPS = 1e-5

ENGINES = ("tensor", "vector", "scalar", "gpsimd", "sync")


def chunk_ranges(core):
    b, c = divmod(core, 4)
    a0 = 512 * c
    b0 = 512 * (7 - c)
    return b, (a0, a0 + 512), (b0, b0 + 512)


def slice_owned(x_full, core):
    b, (a0, a1), (b0, b1) = chunk_ranges(core)
    return np.ascontiguousarray(
        np.concatenate([x_full[b, a0:a1], x_full[b, b0:b1]], axis=0))


def unslice_owned(shards, out):
    for core in range(NCORES):
        b, (a0, a1), (b0, b1) = chunk_ranges(core)
        out[b, a0:a1] = shards[core][:512]
        out[b, b0:b1] = shards[core][512:]
    return out


def pack_slabs(w_in_out, slab_cols):
    IN, OUT = w_in_out.shape
    nk = IN // 128
    ns = OUT // slab_cols
    w = w_in_out.reshape(nk, 128, ns, slab_cols).transpose(2, 1, 0, 3)
    return np.ascontiguousarray(w.reshape(ns, 128, nk * slab_cols))


class Ctx:

    def __init__(self, nc, es):
        self.nc = nc
        self.es = es
        self.s = {}
        self.c = {}
        self.q = {e: [] for e in ENGINES}

    def sems(self, *names):
        for n in names:
            if n not in self.s:
                self.s[n] = self.es.enter_context(self.nc.semaphore(n))
                self.c[n] = 0

    def sb(self, name, shape, dtype):
        return self.es.enter_context(self.nc.sbuf_tensor(name, shape, dtype))

    def ps(self, name, shape, dtype=F32):
        return self.es.enter_context(self.nc.psum_tensor(name, shape, dtype))

    def wait(self, eng, sem, val=None):
        if val is None:
            val = self.c[sem]
        if val <= 0:
            return
        h = self.s[sem]
        self.q[eng].append(lambda e: e.wait_ge(h, val))

    def op(self, eng, fn, sem=None, n=1):
        if sem is None:
            self.q[eng].append(fn)
            return None
        h = self.s[sem]
        self.c[sem] += n
        self.q[eng].append(lambda e: fn(e).then_inc(h, n))
        return self.c[sem]

    def flush(self):
        with self.nc.Block() as block:
            for name in ENGINES:
                thunks = self.q[name]
                if not thunks:
                    continue
                getattr(block, name)(lambda e, th=tuple(thunks): [f(e) for f in th])
        self.q = {e: [] for e in ENGINES}


def alloc_stats(ctx, tag, sq=None):
    return {
        "sq": sq if sq is not None else ctx.sb(f"{tag}_sq", [128, D], BF16),
        "ss": ctx.sb(f"{tag}_ss", [128, 1], F32),
        "sm": ctx.sb(f"{tag}_sm", [128, 1], F32),
        "mu": ctx.sb(f"{tag}_mu", [128, 1], F32),
        "var": ctx.sb(f"{tag}_var", [128, 1], F32),
        "rvar": ctx.sb(f"{tag}_rvar", [128, 1], F32),
        "rstd": ctx.sb(f"{tag}_rstd", [128, 1], F32),
        "nmr": ctx.sb(f"{tag}_nmr", [128, 1], F32),
    }


def emit_ln_stats(ctx, xb, st, sq=None, ready=None):
    V, S = "vector", "scalar"
    sq = st["sq"] if sq is None else sq
    if ready is not None:
        ctx.wait(S, *ready)
        ctx.wait(V, *ready)
    ss_n = ctx.op(S, lambda s: s.activation(out=sq[:], in_=xb[:], func=AF.Square,
                                            accum_out=st["ss"][:]), "act")
    ctx.op(V, lambda v: v.reduce_sum(st["sm"][:], xb[:], axis=AX.X), "dve")
    ctx.wait(V, "dve")
    ctx.op(V, lambda v: v.tensor_scalar(out=st["mu"][:], in0=st["sm"][:],
                                        scalar1=1.0 / D, scalar2=None, op0=ALU.mult), "dve")
    ctx.wait(V, "dve")
    ctx.op(V, lambda v: v.tensor_tensor(out=st["var"][:], in0=st["mu"][:],
                                        in1=st["mu"][:], op=ALU.mult), "dve")
    ctx.wait(V, "dve")
    ctx.wait(V, "act", ss_n)
    ctx.op(V, lambda v: v.scalar_tensor_tensor(
        out=st["var"][:], in0=st["ss"][:], scalar=1.0 / D, in1=st["var"][:],
        op0=ALU.mult, op1=ALU.subtract), "dve")
    ctx.wait(V, "dve")
    ctx.op(V, lambda v: v.tensor_scalar(out=st["var"][:], in0=st["var"][:],
                                        scalar1=LN_EPS, scalar2=None, op0=ALU.add), "dve")
    ctx.wait(V, "dve")
    ctx.op(V, lambda v: v.reciprocal(st["rvar"][:], st["var"][:]), "dve")
    ctx.wait(S, "dve")
    ctx.op(S, lambda s: s.activation(out=st["rstd"][:], in_=st["rvar"][:],
                                     func=AF.Sqrt), "act")
    ctx.wait(V, "act")
    return ctx.op(V, lambda v: v.scalar_tensor_tensor(
        out=st["nmr"][:], in0=st["mu"][:], scalar=-1.0, in1=st["rstd"][:],
        op0=ALU.mult, op1=ALU.mult), "dve")


def build_l0ffn_program():
    nc = bass.Bass("TRN2", target_bir_lowering=False)
    xin = nc.dram_tensor("xin", [TLOC, D], F32, kind="ExternalInput").ap()
    wup = nc.dram_tensor("wup_s", [8, 128, 32 * 512], F32, kind="ExternalInput").ap()
    gate_t = nc.dram_tensor("gate_t", [D, 4], F32, kind="ExternalInput").ap()
    ebias = nc.dram_tensor("ebias", [1, 4], F32, kind="ExternalInput").ap()
    lora_at = nc.dram_tensor("lora_at", [D, 32], F32, kind="ExternalInput").ap()
    lora_bs = nc.dram_tensor("lora_bs", [32, D], F32, kind="ExternalInput").ap()
    t8 = nc.dram_tensor("t8", [4, 32], F32, kind="ExternalInput").ap()
    ident = nc.dram_tensor("ident", [128, 128], F32, kind="ExternalInput").ap()
    xout = nc.dram_tensor("xout", [TLOC, D], F32, kind="ExternalOutput").ap()

    with ExitStack() as es:
        ctx = Ctx(nc, es)
        emit_l0ffn(ctx, xin, xout, wup, gate_t, ebias, lora_at, lora_bs, t8, ident)
        ctx.wait("sync", "dst0")
        ctx.wait("sync", "dst1")
        ctx.flush()
    return nc


def emit_l0ffn(ctx, xin, xout, wup, gate_t, ebias, lora_at, lora_bs, t8, ident):
    ctx.sems("dc", "dx0", "dx1", "dxs0", "dxs1", "dxs2", "dxs3", "dw0", "dw1",
             "pe", "act", "dve", "dst0", "dst1")
    T, V, S, G, Y = "tensor", "vector", "scalar", "gpsimd", "sync"

    gate_sb = ctx.sb("gate_sb", [128, 32, 4], BF16)
    ebias_sb = ctx.sb("ebias_sb", [128, 4], F32)
    la_sb = ctx.sb("la_sb", [128, 32, 32], BF16)
    lb_sb = ctx.sb("lb_sb", [32, D], BF16)
    t8_sb = ctx.sb("t8_sb", [4, 32], BF16)
    id_sb = ctx.sb("id_sb", [128, 128], BF16)
    xb = [ctx.sb(f"xb{i}", [128, D], F32) for i in range(2)]
    xnb = [ctx.sb(f"xnb{i}", [128, D], BF16) for i in range(2)]
    xnt = ctx.sb("xnt", [128, NTB * D], BF16)
    wsl = [ctx.sb(f"wsl{i}", [128, 32 * 512], BF16) for i in range(2)]
    zsel = ctx.sb("zsel", [32, NTB * 128], BF16)
    logits = ctx.sb("logits", [128, 4], F32)
    gmax = ctx.sb("gmax", [128, 1], F32)
    onehot = ctx.sb("onehot", [128, 4], BF16)
    onehT = ctx.sb("onehT", [4, 128], BF16)
    mask_sb = ctx.sb("mask_sb", [32, 128], BF16)
    xsl = [ctx.sb(f"xsl{i}", [128, 512], F32) for i in range(4)]
    xo_sb = [ctx.sb(f"xo{i}", [128, 512], F32) for i in range(2)]
    st = alloc_stats(ctx, "st", sq=xnb[0])

    psT = [ctx.ps(f"psT{i}", [128, 128], BF16) for i in range(2)]
    psg = ctx.ps("psg", [128, 4])
    psoh = ctx.ps("psoh", [4, 128], BF16)
    psm = ctx.ps("psm", [32, 128])
    psz = ctx.ps("psz", [32, 128])
    pso = [ctx.ps(f"pso{i}", [128, 512]) for i in range(2)]

    for fn in (
        lambda g: g.dma_start(out=gate_sb[:], in_=gate_t.rearrange("(k p) e -> p k e", p=128)),
        lambda g: g.dma_start(out=la_sb[:], in_=lora_at.rearrange("(k p) e -> p k e", p=128)),
        lambda g: g.dma_start(out=lb_sb[:], in_=lora_bs[:]),
        lambda g: g.dma_start(out=t8_sb[:], in_=t8[:]),
        lambda g: g.dma_start(out=id_sb[:], in_=ident[:]),
        lambda g: g.dma_start(out=ebias_sb[:], in_=ebias.broadcast_to([128, 4])),
    ):
        ctx.wait(G, "dc")
        ctx.op(G, fn, "dc", 16)
    consts = ctx.c["dc"]

    wsl_ready = [None, None]
    wsl_free = [0, 0]
    ctx.wait(G, "dc", consts)
    ctx.op(G, lambda g: g.dma_start(out=wsl[0][:], in_=wup[0]), "dw0", 16)
    wsl_ready[0] = ("dw0", ctx.c["dw0"])

    xb_ready = [None] * NTB
    ctx.op(Y, lambda sy: sy.dma_start(out=xb[0][:], in_=xin[0:128, :]), "dx0", 16)
    xb_ready[0] = ("dx0", ctx.c["dx0"])
    ctx.op(Y, lambda sy: sy.dma_start(out=xb[1][:], in_=xin[128:256, :]), "dx1", 16)
    xb_ready[1] = ("dx1", ctx.c["dx1"])

    tp_done = [0] * NTB
    gate_done = [0] * NTB
    xnb_free = [0, 0]
    psT_free = [0, 0]
    psg_free = 0
    psz_free = 0
    psoh_free = 0
    psm_free = 0
    onehot_free = 0
    onehT_free = 0
    mask_free = 0

    for tb in range(NTB):
        par = tb % 2
        r0 = tb * 128

        if tb >= 2:
            ctx.wait("scalar", "pe", xnb_free[par])
        emit_ln_stats(ctx, xb[par], st, sq=xnb[par], ready=xb_ready[tb])
        ctx.wait(V, "dve")
        if tb >= 2:
            ctx.wait(V, "pe", xnb_free[par])
        ln_apply = ctx.op(V, lambda v, par=par: v.tensor_scalar(
            out=xnb[par][:], in0=xb[par][:], scalar1=st["rstd"][:],
            scalar2=st["nmr"][:], op0=ALU.mult, op1=ALU.add), "dve")

        if tb + 2 < NTB:
            ctx.wait(Y, "dve", ln_apply)
            ctx.op(Y, lambda sy, tb=tb, par=par: sy.dma_start(
                out=xb[par][:], in_=xin[(tb + 2) * 128:(tb + 3) * 128, :]), f"dx{par}", 16)
            xb_ready[tb + 2] = (f"dx{par}", ctx.c[f"dx{par}"])

        ctx.wait(T, "dc", consts)
        ctx.wait(T, "dve", ln_apply)
        for f in range(32):
            tpar = f % 2
            ctx.wait(T, "dve", psT_free[tpar])
            pe_n = ctx.op(T, lambda t, par=par, tpar=tpar, f=f: t.transpose(
                psT[tpar][:], xnb[par][:, f * 128:(f + 1) * 128], id_sb[:]), "pe")
            ctx.wait(V, "pe", pe_n)
            psT_free[tpar] = ctx.op(V, lambda v, tb=tb, tpar=tpar, f=f: v.tensor_copy(
                xnt[:, tb * D + f * 128: tb * D + (f + 1) * 128], psT[tpar][:]), "dve")
        xnb_free[par] = ctx.c["pe"]
        tp_done[tb] = ctx.c["dve"]

        ctx.wait(T, "dve", max(psg_free, tp_done[tb]))
        for k in range(32):
            ctx.op(T, lambda t, tb=tb, k=k: t.matmul(
                psg[:], xnt[:, tb * D + k * 128:tb * D + (k + 1) * 128],
                gate_sb[:, k, :],
                start=(k == 0), stop=(k == 31)), "pe" if k == 31 else None)
        ctx.wait(V, "pe")
        ctx.wait(V, "dc", consts)
        psg_free = ctx.op(V, lambda v: v.tensor_tensor(out=logits[:], in0=psg[:],
                                            in1=ebias_sb[:], op=ALU.add), "dve")
        ctx.wait(V, "dve")
        ctx.op(V, lambda v: v.reduce_max(gmax[:], logits[:], axis=AX.X), "dve")
        ctx.wait(V, "dve")
        ctx.wait(V, "pe", onehot_free)
        oh_n = ctx.op(V, lambda v: v.tensor_scalar(
            out=onehot[:], in0=logits[:], scalar1=gmax[:], scalar2=None,
            op0=ALU.is_equal), "dve")
        ctx.wait(T, "dve", max(oh_n, psoh_free))
        ohT_pe = ctx.op(T, lambda t: t.transpose(psoh[:], onehot[:], id_sb[:]), "pe")
        onehot_free = ohT_pe
        ctx.wait(V, "pe", max(ohT_pe, onehT_free))
        ohT_n = ctx.op(V, lambda v: v.tensor_copy(onehT[:], psoh[:]), "dve")
        psoh_free = ohT_n
        ctx.wait(T, "dve", ohT_n)
        ctx.wait(T, "act", psm_free)
        m_pe = ctx.op(T, lambda t: t.matmul(psm[:], t8_sb[:], onehT[:],
                                            start=True, stop=True), "pe")
        onehT_free = m_pe
        ctx.wait(S, "pe", m_pe)
        ctx.wait(S, "dve", mask_free)
        msk_n = ctx.op(S, lambda s: s.activation(out=mask_sb[:], in_=psm[:],
                                                 func=AF.Copy), "act")
        psm_free = msk_n
        ctx.wait(T, "dve", psz_free)
        for k in range(32):
            ctx.op(T, lambda t, tb=tb, k=k: t.matmul(
                psz[:], la_sb[:, k, :],
                xnt[:, tb * D + k * 128:tb * D + (k + 1) * 128],
                start=(k == 0), stop=(k == 31)), "pe" if k == 31 else None)
        ctx.wait(V, "pe")
        ctx.wait(V, "act", msk_n)
        gate_done[tb] = ctx.op(V, lambda v, tb=tb: v.tensor_tensor(
            out=zsel[:, tb * 128:(tb + 1) * 128], in0=psz[:], in1=mask_sb[:],
            op=ALU.mult), "dve")
        psz_free = gate_done[tb]
        mask_free = gate_done[tb]

    pso_free = [0, 0]
    xsl_free = [0] * 4
    xo_free = [0, 0]
    it = 0
    for oc in range(8):
        wpar = oc % 2
        if oc + 1 < 8:
            npar = (oc + 1) % 2
            ctx.wait(G, "pe", wsl_free[npar])
            ctx.op(G, lambda g, oc=oc, npar=npar: g.dma_start(
                out=wsl[npar][:], in_=wup[oc + 1]), f"dw{npar}", 16)
            wsl_ready[npar] = (f"dw{npar}", ctx.c[f"dw{npar}"])

        for tb in range(NTB):
            par = it % 2
            xp = it % 4
            ctx.wait(Y, "dve", xsl_free[xp])
            ctx.op(Y, lambda sy, tb=tb, oc=oc, xp=xp: sy.dma_start(
                out=xsl[xp][:], in_=xin[tb * 128:(tb + 1) * 128, oc * 512:(oc + 1) * 512]),
                f"dxs{xp}", 16)
            xsl_n = (f"dxs{xp}", ctx.c[f"dxs{xp}"])

            ctx.wait(T, *wsl_ready[wpar])
            ctx.wait(T, "dve", max(tp_done[tb], pso_free[par]))
            for k in range(32):
                ctx.op(T, lambda t, tb=tb, k=k, par=par, wpar=wpar: t.matmul(
                    pso[par][:], xnt[:, tb * D + k * 128:tb * D + (k + 1) * 128],
                    wsl[wpar][:, k * 512:(k + 1) * 512],
                    start=(k == 0), stop=False))
            ctx.wait(T, "dve", gate_done[tb])
            mm_n = ctx.op(T, lambda t, tb=tb, oc=oc, par=par: t.matmul(
                pso[par][:], zsel[:, tb * 128:(tb + 1) * 128],
                lb_sb[0:32, oc * 512:(oc + 1) * 512],
                start=False, stop=True), "pe")
            if tb == NTB - 1:
                wsl_free[wpar] = mm_n

            ctx.wait(V, "pe", mm_n)
            ctx.wait(V, *xsl_n)
            ctx.wait(V, f"dst{par}", xo_free[par])
            ep_n = ctx.op(V, lambda v, par=par, xp=xp: v.scalar_tensor_tensor(
                out=xo_sb[par][:], in0=pso[par][:], scalar=0.5, in1=xsl[xp][:],
                op0=ALU.mult, op1=ALU.add), "dve")
            pso_free[par] = ep_n
            xsl_free[xp] = ep_n
            ctx.wait(Y, "dve", ep_n)
            xo_free[par] = ctx.op(Y, lambda sy, tb=tb, oc=oc, par=par: sy.dma_start(
                out=xout[tb * 128:(tb + 1) * 128, oc * 512:(oc + 1) * 512],
                in_=xo_sb[par][:]), f"dst{par}", 16)
            it += 1


def stage_l0ffn_weights(inp):
    W_up = np.asarray(inp["L0_ffn_W_up"], np.float32)
    gate = np.asarray(inp["L0_ffn_gate_w"], np.float32)
    eb = np.asarray(inp["L0_ffn_expert_bias"], np.float32)
    A = np.asarray(inp["L0_ffn_lora_A"], np.float32)
    B = np.asarray(inp["L0_ffn_lora_B"], np.float32)
    t8 = np.zeros((4, 32), np.float32)
    for e in range(4):
        t8[e, e * 8:(e + 1) * 8] = 1.0
    return {
        "wup_s": pack_slabs(np.ascontiguousarray(W_up.T), 512),
        "gate_t": np.ascontiguousarray(gate.T),
        "ebias": eb.reshape(1, 4),
        "lora_at": np.ascontiguousarray(A.transpose(2, 0, 1).reshape(D, 32)),
        "lora_bs": np.ascontiguousarray(B.transpose(0, 2, 1).reshape(32, D)),
        "t8": t8,
        "ident": np.eye(128, dtype=np.float32),
    }



TIN = 1028


def xin_row(owned_idx):
    return 1 + owned_idx if owned_idx < 512 else 3 + owned_idx


def sub(ctx, es):
    c = Ctx.__new__(Ctx)
    c.__dict__.update(ctx.__dict__)
    c.es = es
    return c


def build_l0attn_program():
    nc = bass.Bass("TRN2", target_bir_lowering=False)
    xin = nc.dram_tensor("xin_h", [TIN, D], F32, kind="ExternalInput").ap()
    convw = nc.dram_tensor("convw_s", [3, 6, 128, 32 * 512], F32, kind="ExternalInput").ap()
    convb = nc.dram_tensor("convb", [1, 3072], F32, kind="ExternalInput").ap()
    wq = nc.dram_tensor("wq_r", [1024, 1024], F32, kind="ExternalInput").ap()
    wk = nc.dram_tensor("wk_r", [1024, 1024], F32, kind="ExternalInput").ap()
    wv = nc.dram_tensor("wv_r", [1024, 1024], F32, kind="ExternalInput").ap()
    wo = nc.dram_tensor("wo_r", [1024, 1024], F32, kind="ExternalInput").ap()
    bqc = nc.dram_tensor("bq_c", [128, 8], F32, kind="ExternalInput").ap()
    bkc = nc.dram_tensor("bk_c", [128, 8], F32, kind="ExternalInput").ap()
    bvr = nc.dram_tensor("bv_r", [1, 1024], F32, kind="ExternalInput").ap()
    bor = nc.dram_tensor("bo_r", [1, 1024], F32, kind="ExternalInput").ap()
    projw = nc.dram_tensor("projw_s", [8, 128, 8 * 512], F32, kind="ExternalInput").ap()
    projb = nc.dram_tensor("projb", [1, D], F32, kind="ExternalInput").ap()
    ident = nc.dram_tensor("ident", [128, 128], F32, kind="ExternalInput").ap()
    trimask = nc.dram_tensor("trimask", [128, 128], F32, kind="ExternalInput").ap()
    xout = nc.dram_tensor("xout", [TLOC, D], F32, kind="ExternalOutput").ap()

    with ExitStack() as es:
        ctx = Ctx(nc, es)
        emit_l0attn(ctx, xin, xout, convw, convb, wq, wk, wv, wo, bqc, bkc,
                    bvr, bor, projw, projb, ident, trimask)
        ctx.wait("sync", "dst0")
        ctx.wait("sync", "dst1")
        ctx.flush()
    return nc


def emit_l0attn(ctx, xin, xout, convw, convb, wq, wk, wv, wo, bqc, bkc,
                bvr, bor, projw, projb, ident, trimask):
    ctx.sems("dc", "dx0", "dx1", "dxs0", "dxs1", "dxs2", "dxs3",
             "dw0", "dw1", "pe", "act", "dve", "dst0", "dst1")
    T, V, S, G, Y = "tensor", "vector", "scalar", "gpsimd", "sync"

    id_sb = ctx.sb("id_sb", [128, 128], BF16)
    tri_sb = ctx.sb("tri_sb", [128, 128], BF16)
    ones_sb = ctx.sb("ones_sb", [1, 128], BF16)
    convb_sb = ctx.sb("convb_sb", [1, 3072], BF16)
    bq_sb = ctx.sb("bq_sb", [128, 8], F32)
    bk_sb = ctx.sb("bk_sb", [128, 8], F32)
    bv_sb = ctx.sb("bv_sb", [1, 1024], BF16)
    bo_sb = ctx.sb("bo_sb", [1, 1024], BF16)
    projb_sb = ctx.sb("projb_sb", [1, D], BF16)
    qkvc = ctx.sb("qkvc", [128, NTB, 3072], BF16)

    for fn in (
        lambda g: g.dma_start(out=id_sb[:], in_=ident[:]),
        lambda g: g.dma_start(out=tri_sb[:], in_=trimask[:]),
        lambda g: g.dma_start(out=convb_sb[:], in_=convb[:]),
        lambda g: g.dma_start(out=bv_sb[:], in_=bvr[:]),
        lambda g: g.dma_start(out=bo_sb[:], in_=bor[:]),
        lambda g: g.dma_start(out=projb_sb[:], in_=projb[:]),
    ):
        ctx.wait(G, "dc")
        ctx.op(G, fn, "dc", 16)
    ctx.wait(G, "dc")
    ctx.op(G, lambda g: g.dma_start(out=bq_sb[:], in_=bqc[:]), "dc", 16)
    ctx.wait(G, "dc")
    ctx.op(G, lambda g: g.dma_start(out=bk_sb[:], in_=bkc[:]), "dc", 16)
    consts = ctx.c["dc"]
    ctx.op(V, lambda v: v.memset(ones_sb[:], 1.0), "dve")
    ctx.wait(V, "dc", consts)
    ctx.op(V, lambda v: v.tensor_scalar(out=bq_sb[:], in0=bq_sb[:], scalar1=0.125,
                                        scalar2=None, op0=ALU.mult), "dve")
    bq_ready = ctx.c["dve"]
    ones_ready = ctx.c["dve"]

    qkvc_ready = [0] * NTB
    ab0 = [xin_row(128 * i) for i in range(NTB)]

    with ExitStack() as p12:
        c12 = sub(ctx, p12)
        xnt = c12.sb("xnt2", [128, 32, TIN], BF16)

        with ExitStack() as p1:
            c1 = sub(ctx, p1)
            xb = [c1.sb(f"a_xb{i}", [128, D], F32) for i in range(2)]
            xnb = [c1.sb(f"a_xnb{i}", [128, D], BF16) for i in range(2)]
            st = alloc_stats(c1, "a_st", sq=xnb[0])
            psT = [c1.ps(f"a_psT{i}", [128, 128], BF16) for i in range(2)]

            t0s = [min(i * 128, TIN - 128) for i in range(9)]
            xb_ready = [None] * 9
            for i in range(2):
                ctx.op(Y, lambda sy, i=i: sy.dma_start(
                    out=xb[i][:], in_=xin[t0s[i]:t0s[i] + 128, :]), f"dx{i % 2}", 16)
                xb_ready[i] = (f"dx{i % 2}", ctx.c[f"dx{i % 2}"])

            xnb_free = [0, 0]
            psT_free = [0, 0]
            for i in range(9):
                par = i % 2
                if i >= 2:
                    ctx.wait(S, "pe", xnb_free[par])
                emit_ln_stats(ctx, xb[par], st, sq=xnb[par], ready=xb_ready[i])
                ctx.wait(V, "dve")
                if i >= 2:
                    ctx.wait(V, "pe", xnb_free[par])
                ln_apply = ctx.op(V, lambda v, par=par: v.tensor_scalar(
                    out=xnb[par][:], in0=xb[par][:], scalar1=st["rstd"][:],
                    scalar2=st["nmr"][:], op0=ALU.mult, op1=ALU.add), "dve")
                if i + 2 < 9:
                    ctx.wait(Y, "dve", ln_apply)
                    ctx.op(Y, lambda sy, i=i, par=par: sy.dma_start(
                        out=xb[par][:], in_=xin[t0s[i + 2]:t0s[i + 2] + 128, :]),
                        f"dx{par}", 16)
                    xb_ready[i + 2] = (f"dx{par}", ctx.c[f"dx{par}"])
                ctx.wait(T, "dc", consts)
                ctx.wait(T, "dve", ln_apply)
                for f in range(32):
                    tp = f % 2
                    ctx.wait(T, "dve", psT_free[tp])
                    pe_n = ctx.op(T, lambda t, par=par, tp=tp, f=f: t.transpose(
                        psT[tp][:], xnb[par][:, f * 128:(f + 1) * 128], id_sb[:]), "pe")
                    ctx.wait(V, "pe", pe_n)
                    psT_free[tp] = ctx.op(V, lambda v, i=i, tp=tp, f=f: v.tensor_copy(
                        xnt[:, f, t0s[i]:t0s[i] + 128], psT[tp][:]), "dve")
                xnb_free[par] = ctx.c["pe"]
            p1_pe = ctx.c["pe"]
            p1_dve = ctx.c["dve"]
            p1_act = ctx.c["act"]

        with ExitStack() as p2:
            c2 = sub(ctx, p2)
            csl = [c2.sb(f"csl{i}", [128, 32 * 512], BF16) for i in range(2)]
            psc = [c2.ps(f"psc{i}", [128, 512]) for i in range(8)]

            csl_ready = [None, None]
            csl_free = [0, 0]
            ctx.wait(G, "pe", p1_pe)
            ctx.wait(G, "dve", p1_dve)
            ctx.wait(G, "act", p1_act)
            ctx.wait(G, "dc", consts)
            ctx.op(G, lambda g: g.dma_start(out=csl[0][:], in_=convw[0, 0]), "dw0", 16)
            csl_ready[0] = ("dw0", ctx.c["dw0"])

            sl = 0
            for oc6 in range(6):
                for dt in range(3):
                    par = sl % 2
                    if sl + 1 < 18:
                        npar = (sl + 1) % 2
                        noc, ndt = divmod(sl + 1, 3)
                        ctx.wait(G, "pe", csl_free[npar])
                        ctx.op(G, lambda g, noc=noc, ndt=ndt, npar=npar: g.dma_start(
                            out=csl[npar][:], in_=convw[ndt, noc]), f"dw{npar}", 16)
                        csl_ready[npar] = (f"dw{npar}", ctx.c[f"dw{npar}"])
                    ctx.wait(T, *csl_ready[par])
                    if sl == 0:
                        ctx.wait(T, "dve", p1_dve)
                    for ab in range(NTB):
                        if dt == 0:
                            ctx.wait(T, "dve", qkvc_ready[ab])
                        for k in range(32):
                            is_last_of_slab = (ab == NTB - 1 and k == 31)
                            ctx.op(T, lambda t, ab=ab, k=k, dt=dt, par=par:
                                   t.matmul(psc[ab][:],
                                            xnt[:, k, ab0[ab] + dt - 1: ab0[ab] + dt + 127],
                                            csl[par][:, k * 512:(k + 1) * 512],
                                            start=(dt == 0 and k == 0), stop=False),
                                   "pe" if is_last_of_slab else None)
                    csl_free[par] = ctx.c["pe"]
                    sl += 1
                ctx.wait(T, "dve", ones_ready)
                for ab in range(NTB):
                    mm_n = ctx.op(T, lambda t, ab=ab, oc6=oc6: t.matmul(
                        psc[ab][:], ones_sb[:],
                        convb_sb[:, oc6 * 512:(oc6 + 1) * 512],
                        start=False, stop=True), "pe")
                    ctx.wait(V, "pe", mm_n)
                    qkvc_ready[ab] = ctx.op(V, lambda v, ab=ab, oc6=oc6: v.tensor_copy(
                        qkvc[:, ab, oc6 * 512:(oc6 + 1) * 512], psc[ab][:]), "dve")
            p2_pe = ctx.c["pe"]
            p2_dve = ctx.c["dve"]

    o2t_ready = [0] * NTB
    with ExitStack() as p3:
        c3 = sub(ctx, p3)
        o2t = c3.sb("o2t", [128, 8, TLOC], BF16)
        wq_sb = c3.sb("wq_sb", [128, 8, 1024], BF16)
        wk_sb = c3.sb("wk_sb", [128, 8, 1024], BF16)
        wv_sb = c3.sb("wv_sb", [128, 8, 1024], BF16)
        wo_sb = c3.sb("wo_sb", [128, 8, 1024], BF16)
        qkvT = [c3.sb(f"qkvT{i}", [128, 24, 128], BF16) for i in range(2)]
        qhT = [c3.sb(f"qhT{i}", [128, 8, 128], BF16) for i in range(2)]
        khT = [c3.sb(f"khT{i}", [128, 8, 128], BF16) for i in range(2)]
        vh = [c3.sb(f"vh{i}", [128, 16, 65], BF16) for i in range(2)]
        probs = c3.sb("probs", [128, 128], BF16)
        rz = c3.sb("rz", [128, 1], F32)
        oh = [c3.sb(f"oh{i}", [128, 16 * 64], BF16) for i in range(2)]
        o2 = [c3.sb(f"o2{i}", [128, 1024], BF16) for i in range(2)]
        psT3 = [c3.ps(f"psT3_{i}", [128, 128], BF16) for i in range(2)]
        psqk = [c3.ps(f"psqk{i}", [128, 128]) for i in range(2)]
        psv = c3.ps("psv", [128, 512])
        pss = c3.ps("pss", [128, 128])
        pso3 = c3.ps("pso3", [128, 65])
        pswo = c3.ps("pswo", [128, 512])

        for fn in (
            lambda g: g.dma_start(out=wq_sb[:], in_=wq.rearrange("(k p) o -> p k o", p=128)),
            lambda g: g.dma_start(out=wk_sb[:], in_=wk.rearrange("(k p) o -> p k o", p=128)),
            lambda g: g.dma_start(out=wv_sb[:], in_=wv.rearrange("(k p) o -> p k o", p=128)),
            lambda g: g.dma_start(out=wo_sb[:], in_=wo.rearrange("(k p) o -> p k o", p=128)),
        ):
            ctx.wait(G, "pe", p2_pe)
            ctx.wait(G, "dve", p2_dve)
            ctx.wait(G, "dc")
            ctx.op(G, fn, "dc", 16)
        w3_ready = ctx.c["dc"]
        for i in range(2):
            ctx.op(V, lambda v, i=i: v.memset(vh[i][:, :, 64:65], 1.0), "dve")
        vh_ones_dve = ctx.c["dve"]

        qkvT_free = [0, 0]
        psT3_free = [0, 0]
        qhT_free = [0, 0]
        khT_free = [0, 0]
        vh_free = [0, 0]
        oh_free = [0, 0]
        o2_free = [0, 0]
        psqk_free = [0, 0]
        psv_free = 0
        pss_free = 0
        pso3_free = 0
        pswo_free = 0
        probs_free = 0
        rz_free = 0

        ctx.wait(T, "dve", p2_dve)
        for ab in range(NTB):
            par = ab % 2
            ctx.wait(T, "dve", max(qkvc_ready[ab], qkvT_free[par]))
            for f in range(24):
                tp = f % 2
                ctx.wait(T, "dve", psT3_free[tp])
                pe_n = ctx.op(T, lambda t, ab=ab, f=f, tp=tp: t.transpose(
                    psT3[tp][:], qkvc[:, ab, f * 128:(f + 1) * 128], id_sb[:]), "pe")
                ctx.wait(V, "pe", pe_n)
                psT3_free[tp] = ctx.op(V, lambda v, par=par, f=f, tp=tp: v.tensor_copy(
                    qkvT[par][:, f, :], psT3[tp][:]), "dve")
            qkvT_ready = ctx.c["dve"]

            ctx.wait(T, "dc", w3_ready)
            for which, wsb, dsttile, dstfree, bias, sc in (
                    ("q", wq_sb, qhT, qhT_free, bq_sb, 0.125),
                    ("k", wk_sb, khT, khT_free, bk_sb, 1.0)):
                base = 0 if which == "q" else 8
                for hc in range(8):
                    pq = hc % 2
                    ctx.wait(T, "dve", qkvT_ready)
                    ctx.wait(T, "act", psqk_free[pq])
                    for k in range(8):
                        ctx.op(T, lambda t, k=k, hc=hc, par=par, pq=pq,
                               wsb=wsb, base=base: t.matmul(
                            psqk[pq][:], wsb[:, k, hc * 128:(hc + 1) * 128],
                            qkvT[par][:, base + k, :],
                            start=(k == 0), stop=(k == 7)), "pe" if k == 7 else None)
                    ctx.wait(S, "pe", ctx.c["pe"])
                    if which == "q":
                        ctx.wait(S, "dve", bq_ready)
                    if ab >= 2:
                        ctx.wait(S, "pe", dstfree[par])
                    psqk_free[pq] = ctx.op(S, lambda s, hc=hc, par=par, pq=pq,
                                           dsttile=dsttile, bias=bias, sc=sc:
                                           s.activation(
                        out=dsttile[par][:, hc, :], in_=psqk[pq][:], func=AF.Identity,
                        bias=bias[:, hc:hc + 1], scale=sc), "act")
            qk_ready = ctx.c["act"]

            for vc2 in range(2):
                ctx.wait(T, "dve", max(qkvT_ready, ones_ready))
                ctx.wait(T, "act", psv_free)
                for k in range(8):
                    ctx.op(T, lambda t, k=k, vc2=vc2, par=par: t.matmul(
                        psv[:], qkvT[par][:, 16 + k, :],
                        wv_sb[:, k, vc2 * 512:(vc2 + 1) * 512],
                        start=(k == 0), stop=False))
                mm = ctx.op(T, lambda t, vc2=vc2: t.matmul(
                    psv[:], ones_sb[:], bv_sb[:, vc2 * 512:(vc2 + 1) * 512],
                    start=False, stop=True), "pe")
                ctx.wait(S, "pe", mm)
                ctx.wait(S, "dve", vh_ones_dve)
                if ab >= 2:
                    ctx.wait(S, "pe", vh_free[par])
                psv_free = ctx.op(S, lambda s, vc2=vc2, par=par: s.activation(
                    out=vh[par][:, vc2 * 8:(vc2 + 1) * 8, 0:64],
                    in_=psv[:], func=AF.Copy), "act")
            v_ready = ctx.c["act"]

            for h in range(16):
                p0 = 64 * (h % 2)
                kt = h // 2
                ctx.wait(T, "act", max(qk_ready, pss_free))
                s_pe = ctx.op(T, lambda t, par=par, p0=p0, kt=kt: t.matmul(
                    pss[:], khT[par][p0:p0 + 64, kt, :], qhT[par][p0:p0 + 64, kt, :],
                    start=True, stop=True), "pe")
                ctx.wait(S, "pe", max(s_pe, probs_free))
                e_n = ctx.op(S, lambda s: s.activation(
                    out=probs[:], in_=pss[:], func=AF.Exp), "act")
                pss_free = e_n
                ctx.wait(V, "act", e_n)
                m_n = ctx.op(V, lambda v: v.tensor_tensor(
                    out=probs[:], in0=probs[:], in1=tri_sb[:], op=ALU.mult), "dve")
                ctx.wait(T, "dve", m_n)
                ctx.wait(T, "act", max(v_ready, pso3_free))
                pv_pe = ctx.op(T, lambda t, h=h, par=par: t.matmul(
                    pso3[:], probs[:], vh[par][:, h, :], start=True, stop=True), "pe")
                probs_free = pv_pe
                ctx.wait(V, "pe", pv_pe)
                ctx.wait(V, "act", rz_free)
                r_n = ctx.op(V, lambda v: v.reciprocal(rz[:], pso3[:, 64:65]), "dve")
                ctx.wait(S, "dve", r_n)
                if ab >= 2:
                    ctx.wait(S, "pe", oh_free[par])
                pso3_free = ctx.op(S, lambda s, h=h, par=par: s.activation(
                    out=oh[par][:, h * 64:(h + 1) * 64], in_=pso3[:, 0:64],
                    func=AF.Copy, scale=rz[:]), "act")
                rz_free = pso3_free
            oh_ready = ctx.c["act"]

            ctx.wait(T, "act", oh_ready)
            for f in range(8):
                tp = f % 2
                ctx.wait(T, "dve", psT3_free[tp])
                pe_n = ctx.op(T, lambda t, par=par, f=f, tp=tp: t.transpose(
                    psT3[tp][:], oh[par][:, f * 128:(f + 1) * 128], id_sb[:]), "pe")
                ctx.wait(V, "pe", pe_n)
                psT3_free[tp] = ctx.op(V, lambda v, par=par, f=f, tp=tp: v.tensor_copy(
                    qkvT[par][:, f, :], psT3[tp][:]), "dve")
            ohT_ready = ctx.c["dve"]
            oh_free[par] = ctx.c["pe"]

            for wc in range(2):
                ctx.wait(T, "dve", max(ohT_ready, ones_ready))
                ctx.wait(T, "act", pswo_free)
                for k in range(8):
                    ctx.op(T, lambda t, k=k, wc=wc, par=par: t.matmul(
                        pswo[:], qkvT[par][:, k, :],
                        wo_sb[:, k, wc * 512:(wc + 1) * 512],
                        start=(k == 0), stop=False))
                mm = ctx.op(T, lambda t, wc=wc: t.matmul(
                    pswo[:], ones_sb[:], bo_sb[:, wc * 512:(wc + 1) * 512],
                    start=False, stop=True), "pe")
                ctx.wait(S, "pe", mm)
                if ab >= 2:
                    ctx.wait(S, "pe", o2_free[par])
                pswo_free = ctx.op(S, lambda s, wc=wc, par=par: s.activation(
                    out=o2[par][:, wc * 512:(wc + 1) * 512], in_=pswo[:],
                    func=AF.Copy), "act")
            o2_ready = ctx.c["act"]

            ctx.wait(T, "act", o2_ready)
            for f in range(8):
                tp = f % 2
                ctx.wait(T, "dve", psT3_free[tp])
                pe_n = ctx.op(T, lambda t, par=par, f=f, tp=tp: t.transpose(
                    psT3[tp][:], o2[par][:, f * 128:(f + 1) * 128], id_sb[:]), "pe")
                ctx.wait(V, "pe", pe_n)
                psT3_free[tp] = ctx.op(V, lambda v, ab=ab, f=f, tp=tp: v.tensor_copy(
                    o2t[:, f, ab * 128:(ab + 1) * 128], psT3[tp][:]), "dve")
            o2t_ready[ab] = ctx.c["dve"]
            o2_free[par] = ctx.c["pe"]
            qkvT_free[par] = ctx.c["pe"]
            qhT_free[par] = ctx.c["pe"]
            khT_free[par] = ctx.c["pe"]
            vh_free[par] = ctx.c["pe"]
        p3_pe = ctx.c["pe"]
        p3_dve = ctx.c["dve"]
        p3_act = ctx.c["act"]

        with ExitStack() as p4:
            c4 = sub(ctx, p4)
            pjs = [c4.sb(f"pjs{i}", [128, 8 * 512], BF16) for i in range(2)]
            xsl = [c4.sb(f"p_xsl{i}", [128, 512], F32) for i in range(4)]
            xo_sb = [c4.sb(f"p_xo{i}", [128, 512], F32) for i in range(2)]
            psp = [psv, pswo]

            pjs_ready = [None, None]
            pjs_free = [0, 0]
            ctx.wait(T, "act", p3_act)
            ctx.wait(T, "dve", p3_dve)
            ctx.wait(G, "pe", p3_pe)
            ctx.wait(G, "dve", p3_dve)
            ctx.wait(G, "act", p3_act)
            ctx.wait(G, "dw0")
            ctx.wait(G, "dw1")
            ctx.op(G, lambda g: g.dma_start(out=pjs[0][:], in_=projw[0]), "dw0", 16)
            pjs_ready[0] = ("dw0", ctx.c["dw0"])

            psp_free = [0, 0]
            xsl_free = [0] * 4
            xo_free = [0, 0]
            it = 0
            for oc in range(8):
                wp = oc % 2
                if oc + 1 < 8:
                    npar = (oc + 1) % 2
                    ctx.wait(G, "pe", pjs_free[npar])
                    ctx.op(G, lambda g, oc=oc, npar=npar: g.dma_start(
                        out=pjs[npar][:], in_=projw[oc + 1]), f"dw{npar}", 16)
                    pjs_ready[npar] = (f"dw{npar}", ctx.c[f"dw{npar}"])
                for tb in range(NTB):
                    par = it % 2
                    xp = it % 4
                    ctx.wait(Y, "dve", xsl_free[xp])
                    r0 = xin_row(tb * 128)
                    ctx.op(Y, lambda sy, oc=oc, xp=xp, r0=r0: sy.dma_start(
                        out=xsl[xp][:], in_=xin[r0:r0 + 128, oc * 512:(oc + 1) * 512]),
                        f"dxs{xp}", 16)
                    xsl_n = (f"dxs{xp}", ctx.c[f"dxs{xp}"])

                    ctx.wait(T, *pjs_ready[wp])
                    ctx.wait(T, "dve", max(o2t_ready[tb], psp_free[par], ones_ready))
                    for k in range(8):
                        ctx.op(T, lambda t, tb=tb, k=k, par=par, wp=wp: t.matmul(
                            psp[par][:], o2t[:, k, tb * 128:(tb + 1) * 128],
                            pjs[wp][:, k * 512:(k + 1) * 512],
                            start=(k == 0), stop=False))
                    mm_n = ctx.op(T, lambda t, oc=oc, par=par: t.matmul(
                        psp[par][:], ones_sb[:], projb_sb[:, oc * 512:(oc + 1) * 512],
                        start=False, stop=True), "pe")
                    if tb == NTB - 1:
                        pjs_free[wp] = mm_n

                    ctx.wait(V, "pe", mm_n)
                    ctx.wait(V, *xsl_n)
                    ctx.wait(V, f"dst{par}", xo_free[par])
                    ep_n = ctx.op(V, lambda v, par=par, xp=xp: v.tensor_tensor(
                        out=xo_sb[par][:], in0=psp[par][:], in1=xsl[xp][:],
                        op=ALU.add), "dve")
                    psp_free[par] = ep_n
                    xsl_free[xp] = ep_n
                    ctx.wait(Y, "dve", ep_n)
                    xo_free[par] = ctx.op(Y, lambda sy, tb=tb, oc=oc, par=par: sy.dma_start(
                        out=xout[tb * 128:(tb + 1) * 128, oc * 512:(oc + 1) * 512],
                        in_=xo_sb[par][:]), f"dst{par}", 16)
                    it += 1


def stage_l0attn_weights(inp):
    cw = np.asarray(inp["L0_attn_conv_w"], np.float32)
    convw = np.stack([pack_slabs(np.ascontiguousarray(cw[dt]), 512) for dt in range(3)])
    wq = np.ascontiguousarray(np.asarray(inp["L0_attn_wq"], np.float32).reshape(1024, 1024))
    wk = np.ascontiguousarray(np.asarray(inp["L0_attn_wk"], np.float32).reshape(1024, 1024))
    wv = np.ascontiguousarray(np.asarray(inp["L0_attn_wv"], np.float32).reshape(1024, 1024))
    wo = np.ascontiguousarray(np.asarray(inp["L0_attn_wo"], np.float32).reshape(1024, 1024))
    bq = np.asarray(inp["L0_attn_bq"], np.float32).reshape(1024)
    bk = np.asarray(inp["L0_attn_bk"], np.float32).reshape(1024)
    bv = np.asarray(inp["L0_attn_bv"], np.float32).reshape(1024)
    bo = np.asarray(inp["L0_attn_bo"], np.float32)
    pw = np.asarray(inp["L0_attn_proj_w"], np.float32)
    pb = np.asarray(inp["L0_attn_proj_b"], np.float32)
    cb = np.asarray(inp["L0_attn_conv_b"], np.float32)
    k_idx, q_idx = np.meshgrid(np.arange(128), np.arange(128), indexing="ij")
    tri_kq = (k_idx <= q_idx).astype(np.float32)
    return {
        "convw_s": convw,
        "convb": cb.reshape(1, 3072),
        "wq_r": wq, "wk_r": wk, "wv_r": wv, "wo_r": wo,
        "bq_c": np.ascontiguousarray(bq.reshape(8, 128).T),
        "bk_c": np.ascontiguousarray(bk.reshape(8, 128).T),
        "bv_r": bv.reshape(1, 1024),
        "bo_r": bo.reshape(1, 1024),
        "projw_s": pack_slabs(pw, 512),
        "projb": pb.reshape(1, D),
        "ident": np.eye(128, dtype=np.float32),
        "trimask": tri_kq,
    }


def slice_haloed(x_full, core):
    b, (a0, a1), (b0, b1) = chunk_ranges(core)
    out = np.zeros((TIN, D), np.float32)
    out[1:513] = x_full[b, a0:a1]
    out[515:1027] = x_full[b, b0:b1]
    if a0 > 0:
        out[0] = x_full[b, a0 - 1]
    if a1 < 4096:
        out[513] = x_full[b, a1]
    if b0 > 0:
        out[514] = x_full[b, b0 - 1]
    if b1 < 4096:
        out[1027] = x_full[b, b1]
    return out


def build_l1ffn_program():
    nc = bass.Bass("TRN2", target_bir_lowering=False)
    xin = nc.dram_tensor("xin", [TLOC, D], F32, kind="ExternalInput").ap()
    a2 = nc.dram_tensor("a2", [128, 64], F32, kind="ExternalInput").ap()
    b2 = nc.dram_tensor("b2", [128, 64], F32, kind="ExternalInput").ap()
    betar = nc.dram_tensor("betar", [64, 64], F32, kind="ExternalInput").ap()
    gcols = nc.dram_tensor("gcols", [128, 32], F32, kind="ExternalInput").ap()
    biasr = nc.dram_tensor("biasr", [1, D], F32, kind="ExternalInput").ap()
    ident = nc.dram_tensor("ident", [128, 128], F32, kind="ExternalInput").ap()
    xout = nc.dram_tensor("xout", [TLOC, D], F32, kind="ExternalOutput").ap()
    cdram = nc.dram_tensor("c_scratch", [64, 64], F32).ap()

    with ExitStack() as es:
        ctx = Ctx(nc, es)
        emit_l1ffn(ctx, xin, xout, a2, b2, betar, gcols, biasr, ident, cdram)
        ctx.wait("sync", "dst0")
        ctx.wait("sync", "dst1")
        ctx.flush()
    return nc


def emit_l1ffn(ctx, xin, xout, a2, b2, betar, gcols, biasr, ident, cdram):
    ctx.sems("dc", "dcy", "dcy2", "dx0", "dx1", "dw0", "pe", "act", "dve", "dst0", "dst1")
    T, V, S, G, Y = "tensor", "vector", "scalar", "gpsimd", "sync"

    id_sb = ctx.sb("f_id", [128, 128], BF16)
    a2_sb = ctx.sb("f_a2", [128, 64], BF16)
    b2_sb = ctx.sb("f_b2", [128, 64], BF16)
    betar_sb = ctx.sb("f_betar", [64, 64], BF16)
    gc_sb = ctx.sb("f_gc", [128, 32], F32)
    bias_bc = ctx.sb("f_biasbc", [128, D], F32)
    cb2 = ctx.sb("f_cb2", [128, D], BF16)
    m1_sb = ctx.sb("f_m1", [64, 64], BF16)
    m1t_sb = ctx.sb("f_m1t", [64, 64], BF16)
    c_sb = ctx.sb("f_c", [64, 64], F32)

    xb = [ctx.sb(f"f_xb{i}", [128, D], F32) for i in range(2)]
    xn2 = [ctx.sb(f"f_xn2{i}", [128, D], BF16) for i in range(2)]
    hT = [ctx.sb(f"f_hT{i}", [128, 32, 128], BF16) for i in range(2)]
    u_sb = [ctx.sb(f"f_u{i}", [128, D], BF16) for i in range(2)]
    uT = [ctx.sb(f"f_uT{i}", [128, 32, 128], BF16) for i in range(2)]
    y_sb = [ctx.sb(f"f_y{i}", [128, D], F32) for i in range(2)]
    g_sb = ctx.sb("f_g", [128, D], F32)
    e1_sb = ctx.sb("f_e1", [128, D], F32)
    xo_sb = y_sb
    st = alloc_stats(ctx, "f_st", sq=xn2[0])
    r1c = ctx.sb("f_r1", [128, 1], F32)
    v2e = ctx.sb("f_v2e", [128, 1], F32)
    r2c = ctx.sb("f_r2", [128, 1], F32)
    rstd2 = ctx.sb("f_rstd2", [128, 1], F32)
    s2col = ctx.sb("f_s2col", [128, 1], F32)
    Rc = ctx.sb("f_R", [128, 1], F32)
    nmR = ctx.sb("f_nmR", [128, 1], F32)

    psT = [ctx.ps(f"f_psT{i}", [128, 128], BF16) for i in range(2)]
    psu = [ctx.ps(f"f_psu{i}", [128, 64]) for i in range(2)]
    psy = [ctx.ps(f"f_psy{i}", [128, 64]) for i in range(2)]
    psc1 = ctx.ps("f_psc1", [64, 64])
    psc2 = ctx.ps("f_psc2", [64, 64], BF16)

    for fn in (
        lambda g: g.dma_start(out=id_sb[:], in_=ident[:]),
        lambda g: g.dma_start(out=a2_sb[:], in_=a2[:]),
        lambda g: g.dma_start(out=b2_sb[:], in_=b2[:]),
        lambda g: g.dma_start(out=betar_sb[:], in_=betar[:]),
        lambda g: g.dma_start(out=gc_sb[:], in_=gcols[:]),
    ):
        ctx.wait(G, "dc")
        ctx.op(G, fn, "dc", 16)
    consts = ctx.c["dc"]
    ctx.op(Y, lambda sy: sy.dma_start(out=bias_bc[:], in_=biasr.broadcast_to([128, D])), "dcy", 16)
    bias_ready = ("dcy", ctx.c["dcy"])

    ctx.wait(T, "dc", consts)
    n = ctx.op(T, lambda t: t.matmul(psc1[:], b2_sb[0:64, :], betar_sb[:],
                                     start=True, stop=True), "pe")
    ctx.wait(V, "pe", n)
    n = ctx.op(V, lambda v: v.tensor_copy(m1_sb[:], psc1[:]), "dve")
    ctx.wait(T, "dve", n)
    n = ctx.op(T, lambda t: t.transpose(psc2[:], m1_sb[:], id_sb[0:64, 0:64]), "pe")
    ctx.wait(V, "pe", n)
    n = ctx.op(V, lambda v: v.tensor_copy(m1t_sb[:], psc2[:]), "dve")
    ctx.wait(T, "dve", n)
    n = ctx.op(T, lambda t: t.matmul(psc1[:], m1t_sb[:], a2_sb[0:64, :],
                                     start=True, stop=True), "pe")
    ctx.wait(V, "pe", n)
    n = ctx.op(V, lambda v: v.tensor_copy(c_sb[:], psc1[:]), "dve")
    ctx.wait(Y, "dve", n)
    ctx.op(Y, lambda sy: sy.dma_start(out=cdram[:], in_=c_sb[:]), "dcy2", 16)
    ctx.wait(G, "dcy2")
    ctx.wait(G, "dc", consts)
    ctx.op(G, lambda g: g.dma_start(
        out=cb2[:], in_=cdram.rearrange("q b -> (q b)").unsqueeze(0).broadcast_to([128, D])), "dc", 16)
    cb2_ready = ("dc", ctx.c["dc"])

    xb_ready = [None] * NTB
    for i in range(2):
        ctx.op(Y, lambda sy, i=i: sy.dma_start(
            out=xb[i][:], in_=xin[i * 128:(i + 1) * 128, :]), f"dx{i % 2}", 16)
        xb_ready[i] = (f"dx{i % 2}", ctx.c[f"dx{i % 2}"])

    xn2_free = [0, 0]
    hT_free = [0, 0]
    uT_free = [0, 0]
    u_free = [0, 0]
    psT_free = [0, 0]
    psT_free_sem = [None, None]
    psu_free = [0, 0]
    psy_free = [0, 0]
    y_free = [0, 0]
    xo_free = [0, 0]
    xb_free = [0, 0]
    xn2_freed_dve = [0, 0]

    for tb in range(NTB):
        par = tb % 2
        if tb >= 2:
            ctx.wait(S, "pe", xn2_free[par])
            ctx.wait(S, "dve", xn2_freed_dve[par])
        emit_ln_stats(ctx, xb[par], st, sq=xn2[par], ready=xb_ready[tb])
        ctx.wait(V, "dve")
        ctx.op(V, lambda v: v.tensor_scalar(out=v2e[:], in0=st["rvar"][:],
                                            scalar1=-LN_EPS, scalar2=1.0 + LN_EPS,
                                            op0=ALU.mult, op1=ALU.add), "dve")
        ctx.wait(V, "dve")
        ctx.op(V, lambda v: v.reciprocal(r2c[:], v2e[:]), "dve")
        ctx.wait(S, "dve")
        ctx.op(S, lambda s: s.activation(out=rstd2[:], in_=r2c[:], func=AF.Sqrt), "act")
        ctx.op(S, lambda s: s.activation(out=s2col[:], in_=v2e[:], func=AF.Sqrt), "act")
        ctx.wait(V, "act")
        ctx.op(V, lambda v: v.tensor_tensor(out=Rc[:], in0=st["rstd"][:],
                                            in1=rstd2[:], op=ALU.mult), "dve")
        ctx.wait(V, "dve")
        ctx.op(V, lambda v: v.scalar_tensor_tensor(
            out=nmR[:], in0=st["mu"][:], scalar=-1.0, in1=Rc[:],
            op0=ALU.mult, op1=ALU.mult), "dve")
        ctx.wait(V, "dve")
        if tb >= 2:
            ctx.wait(V, "pe", xn2_free[par])
        xn2_n = ctx.op(V, lambda v, par=par: v.tensor_scalar(
            out=xn2[par][:], in0=xb[par][:], scalar1=Rc[:], scalar2=nmR[:],
            op0=ALU.mult, op1=ALU.add), "dve")

        ctx.wait(T, "dc", consts)
        ctx.wait(T, "dve", xn2_n)
        for f in range(32):
            tp = f % 2
            if psT_free_sem[tp]:
                ctx.wait(T, psT_free_sem[tp], psT_free[tp])
            if tb >= 2 and f < 2:
                ctx.wait(T, "pe", hT_free[par])
            pe_n = ctx.op(T, lambda t, par=par, tp=tp, f=f: t.transpose(
                psT[tp][:], xn2[par][:, f * 128:(f + 1) * 128], id_sb[:]), "pe")
            ctx.wait(S, "pe", pe_n)
            if tb >= 2:
                ctx.wait(S, "pe", hT_free[par])
            psT_free[tp] = ctx.op(S, lambda s, par=par, tp=tp, f=f: s.activation(
                out=hT[par][:, f, :], in_=psT[tp][:], func=AF.Identity,
                scale=gc_sb[:, f:f + 1]), "act")
            psT_free_sem[tp] = "act"
        xn2_free[par] = ctx.c["pe"]
        hT_ready = ctx.c["act"]

        ctx.wait(T, "act", hT_ready)
        for p in range(64):
            pp = p % 2
            base = 64 * (p % 2)
            f = p // 2
            ctx.wait(T, "dve", psu_free[pp])
            mm = ctx.op(T, lambda t, par=par, f=f, base=base, pp=pp: t.matmul(
                psu[pp][:], hT[par][base:base + 64, f, :], a2_sb[base:base + 64, :],
                start=True, stop=True), "pe")
            ctx.wait(V, "pe", mm)
            psu_free[pp] = ctx.op(V, lambda v, par=par, p=p, pp=pp: v.tensor_copy(
                u_sb[par][:, p::64], psu[pp][:]), "dve")
        u_ready = ctx.c["dve"]

        ctx.wait(T, "dve", u_ready)
        for f in range(32):
            tp = f % 2
            ctx.wait(T, psT_free_sem[tp], psT_free[tp])
            if tb >= 2 and f < 2:
                ctx.wait(T, "pe", uT_free[par])
            pe_n = ctx.op(T, lambda t, par=par, tp=tp, f=f: t.transpose(
                psT[tp][:], u_sb[par][:, f * 128:(f + 1) * 128], id_sb[:]), "pe")
            ctx.wait(V, "pe", pe_n)
            if tb >= 2:
                ctx.wait(V, "pe", uT_free[par])
            psT_free[tp] = ctx.op(V, lambda v, par=par, tp=tp, f=f: v.tensor_copy(
                uT[par][:, f, :], psT[tp][:]), "dve")
            psT_free_sem[tp] = "dve"
        uT_ready = ctx.c["dve"]
        u_freed_dve_cur = ctx.c["pe"]

        ctx.wait(T, "dve", uT_ready)
        ctx.wait(V, *cb2_ready)
        for b in range(64):
            pp = b % 2
            base = 64 * (b % 2)
            f = b // 2
            ctx.wait(T, "dve", psy_free[pp])
            mm = ctx.op(T, lambda t, par=par, f=f, base=base, pp=pp: t.matmul(
                psy[pp][:], uT[par][base:base + 64, f, :], b2_sb[base:base + 64, :],
                start=True, stop=True), "pe")
            ctx.wait(V, "pe", mm)
            if b < 2 and tb >= 2:
                ctx.wait(V, "act", y_free[par])
                ctx.wait(V, f"dst{par}", xo_free[par])
            psy_free[pp] = ctx.op(V, lambda v, par=par, b=b, pp=pp: v.scalar_tensor_tensor(
                out=y_sb[par][:, b::64], in0=psy[pp][:], scalar=1.0,
                in1=cb2[:, b::64], op0=ALU.mult, op1=ALU.add), "dve")
        y_ready = ctx.c["dve"]

        ctx.wait(S, "dve", y_ready)
        gelu_n = ctx.op(S, lambda s, par=par: s.activation(
            out=g_sb[:], in_=y_sb[par][:], func=AF.Gelu), "act")
        y_free[par] = gelu_n
        ctx.wait(V, "act", gelu_n)
        ctx.wait(V, *bias_ready)
        e1_n = ctx.op(V, lambda v: v.tensor_tensor(
            out=e1_sb[:], in0=g_sb[:], in1=bias_bc[:], op=ALU.add), "dve")
        ctx.wait(V, "dve", e1_n)
        e2_n = ctx.op(V, lambda v, par=par: v.scalar_tensor_tensor(
            out=e1_sb[:], in0=xn2[par][:], scalar=s2col[:], in1=e1_sb[:],
            op0=ALU.mult, op1=ALU.add), "dve")
        ctx.wait(V, "dve", e2_n)
        e3_n = ctx.op(V, lambda v, par=par: v.scalar_tensor_tensor(
            out=xo_sb[par][:], in0=e1_sb[:], scalar=0.5, in1=xb[par][:],
            op0=ALU.mult, op1=ALU.add), "dve")
        xb_free[par] = e3_n
        ctx.wait(Y, "dve", e3_n)
        xo_free[par] = ctx.op(Y, lambda sy, tb=tb, par=par: sy.dma_start(
            out=xout[tb * 128:(tb + 1) * 128, :], in_=xo_sb[par][:]), f"dst{par}", 16)

        if tb + 2 < NTB:
            ctx.wait(Y, "dve", e3_n)
            ctx.op(Y, lambda sy, tb=tb, par=par: sy.dma_start(
                out=xb[par][:], in_=xin[(tb + 2) * 128:(tb + 3) * 128, :]), f"dx{par}", 16)
            xb_ready[tb + 2] = (f"dx{par}", ctx.c[f"dx{par}"])

        hT_free[par] = ctx.c["pe"]
        uT_free[par] = ctx.c["pe"]
        xn2_freed_dve[par] = e2_n


def stage_l1ffn_weights(inp):
    A = np.asarray(inp["L1_ffn_A"], np.float32)
    B = np.asarray(inp["L1_ffn_B"], np.float32)
    beta = np.asarray(inp["L1_ffn_ln_beta"], np.float32)
    gamma = np.asarray(inp["L1_ffn_ln_gamma"], np.float32)
    bias = np.asarray(inp["L1_ffn_bias"], np.float32)
    return {
        "a2": np.ascontiguousarray(np.concatenate([A, A], axis=0)),
        "b2": np.ascontiguousarray(np.concatenate([B, B], axis=0)),
        "betar": np.ascontiguousarray(beta.reshape(64, 64)),
        "gcols": np.ascontiguousarray(gamma.reshape(32, 128).T),
        "biasr": bias.reshape(1, D),
        "ident": np.eye(128, dtype=np.float32),
    }


def build_l1attn_a_program():
    nc = bass.Bass("TRN2", target_bir_lowering=False)
    xin = nc.dram_tensor("xin", [TLOC, D], F32, kind="ExternalInput").ap()
    wqs = nc.dram_tensor("wq_s", [8, 128, 32 * 512], F32, kind="ExternalInput").ap()
    wks = nc.dram_tensor("wk_s", [8, 128, 32 * 512], F32, kind="ExternalInput").ap()
    wvs = nc.dram_tensor("wv_s", [8, 128, 32 * 512], F32, kind="ExternalInput").ap()
    cs = nc.dram_tensor("cs_cols", [TLOC, 6], F32, kind="ExternalInput").ap()
    ident = nc.dram_tensor("ident", [128, 128], F32, kind="ExternalInput").ap()
    qt = nc.dram_tensor("qt", [128, 32, TLOC], BF16, kind="ExternalOutput").ap()
    kt = nc.dram_tensor("kt", [128, 32, TLOC], BF16, kind="ExternalOutput").ap()
    vt = nc.dram_tensor("vt", [TLOC, D], BF16, kind="ExternalOutput").ap()

    with ExitStack() as es:
        ctx = Ctx(nc, es)
        emit_l1attn_a(ctx, xin, wqs, wks, wvs, cs, ident, qt, kt, vt)
        ctx.wait("sync", "dst0")
        ctx.wait("sync", "dst1")
        ctx.flush()
    return nc


def emit_l1attn_a(ctx, xin, wqs, wks, wvs, cs, ident, qt, kt, vt):
    ctx.sems("dc", "dx0", "dx1", "dw0", "dw1", "pe", "act", "dve", "dst0", "dst1")
    T, V, S, G, Y = "tensor", "vector", "scalar", "gpsimd", "sync"

    id_sb = ctx.sb("q_id", [128, 128], BF16)
    cs_sb = ctx.sb("q_cs", [128, NTB, 6], F32)
    xnt = ctx.sb("q_xnt", [128, 32, TLOC], BF16)
    wsl = [ctx.sb(f"q_wsl{i}", [128, 32 * 512], BF16) for i in range(2)]
    ro_sb = [ctx.sb(f"q_ro{i}", [128, 512], BF16) for i in range(2)]
    tt_sb = [ctx.sb(f"q_tt{i}", [128, 128], BF16) for i in range(2)]

    psT = [ctx.ps(f"q_psT{i}", [128, 128], BF16) for i in range(2)]
    pso = [ctx.ps(f"q_pso{i}", [128, 512]) for i in range(2)]

    ctx.wait(G, "dc")
    ctx.op(G, lambda g: g.dma_start(out=id_sb[:], in_=ident[:]), "dc", 16)
    ctx.wait(G, "dc")
    ctx.op(G, lambda g: g.dma_start(
        out=cs_sb[:], in_=cs.rearrange("(tb p) c -> p tb c", p=128)), "dc", 16)
    consts = ctx.c["dc"]

    with ExitStack() as p1:
        c1 = sub(ctx, p1)
        xb = [c1.sb(f"q_xb{i}", [128, D], F32) for i in range(2)]
        xnb = [c1.sb(f"q_xnb{i}", [128, D], BF16) for i in range(2)]
        st = alloc_stats(c1, "q_st", sq=xnb[0])

        xb_ready = [None] * NTB
        for i in range(2):
            ctx.op(Y, lambda sy, i=i: sy.dma_start(
                out=xb[i][:], in_=xin[i * 128:(i + 1) * 128, :]), f"dx{i % 2}", 16)
            xb_ready[i] = (f"dx{i % 2}", ctx.c[f"dx{i % 2}"])
        xnb_free = [0, 0]
        psT_free = [0, 0]
        for i in range(NTB):
            par = i % 2
            if i >= 2:
                ctx.wait(S, "pe", xnb_free[par])
            emit_ln_stats(ctx, xb[par], st, sq=xnb[par], ready=xb_ready[i])
            ctx.wait(V, "dve")
            if i >= 2:
                ctx.wait(V, "pe", xnb_free[par])
            ln_apply = ctx.op(V, lambda v, par=par: v.tensor_scalar(
                out=xnb[par][:], in0=xb[par][:], scalar1=st["rstd"][:],
                scalar2=st["nmr"][:], op0=ALU.mult, op1=ALU.add), "dve")
            if i + 2 < NTB:
                ctx.wait(Y, "dve", ln_apply)
                ctx.op(Y, lambda sy, i=i, par=par: sy.dma_start(
                    out=xb[par][:], in_=xin[(i + 2) * 128:(i + 3) * 128, :]), f"dx{par}", 16)
                xb_ready[i + 2] = (f"dx{par}", ctx.c[f"dx{par}"])
            ctx.wait(T, "dc", consts)
            ctx.wait(T, "dve", ln_apply)
            for f in range(32):
                tp = f % 2
                ctx.wait(T, "dve", psT_free[tp])
                pe_n = ctx.op(T, lambda t, par=par, tp=tp, f=f: t.transpose(
                    psT[tp][:], xnb[par][:, f * 128:(f + 1) * 128], id_sb[:]), "pe")
                ctx.wait(V, "pe", pe_n)
                psT_free[tp] = ctx.op(V, lambda v, i=i, tp=tp, f=f: v.tensor_copy(
                    xnt[:, f, i * 128:(i + 1) * 128], psT[tp][:]), "dve")
            xnb_free[par] = ctx.c["pe"]
        p1_pe = ctx.c["pe"]
        p1_dve = ctx.c["dve"]
        p1_act = ctx.c["act"]

    wsl_ready = [None, None]
    wsl_free = [0, 0]
    pso_free = [0, 0]
    ro_free = [0, 0]
    ro_free_sem = [None, None]
    tt_free = [0, 0]
    psT2_free = [0, 0]

    ctx.wait(G, "pe", p1_pe)
    ctx.wait(G, "dve", p1_dve)
    ctx.wait(G, "act", p1_act)

    sl = 0
    for wi, (wsrc, dst) in enumerate(((wqs, qt), (wks, kt), (wvs, vt))):
        is_v = wi == 2
        csel = (0, 3, None)[wi]
        for oc in range(8):
            par = sl % 2
            if sl == 0:
                ctx.wait(G, "dc", consts)
                ctx.op(G, lambda g: g.dma_start(out=wsl[0][:], in_=wqs[0]), "dw0", 16)
                wsl_ready[0] = ("dw0", ctx.c["dw0"])
            if sl + 1 < 24:
                npar = (sl + 1) % 2
                nwi, noc = divmod(sl + 1, 8)
                nsrc = (wqs, wks, wvs)[nwi]
                ctx.wait(G, "pe", wsl_free[npar])
                ctx.op(G, lambda g, nsrc=nsrc, noc=noc, npar=npar: g.dma_start(
                    out=wsl[npar][:], in_=nsrc[noc]), f"dw{npar}", 16)
                wsl_ready[npar] = (f"dw{npar}", ctx.c[f"dw{npar}"])

            for tb in range(NTB):
                pp = (sl * NTB + tb) % 2
                ctx.wait(T, *wsl_ready[par])
                ctx.wait(T, "dve", pso_free[pp])
                for k in range(32):
                    mm = ctx.op(T, lambda t, tb=tb, k=k, par=par, pp=pp: t.matmul(
                        pso[pp][:], xnt[:, k, tb * 128:(tb + 1) * 128],
                        wsl[par][:, k * 512:(k + 1) * 512],
                        start=(k == 0), stop=(k == 31)), "pe" if k == 31 else None)
                if tb == NTB - 1:
                    wsl_free[par] = mm

                if is_v:
                    ctx.wait(V, "pe", mm)
                    rpar = pp
                    if ro_free_sem[rpar]:
                        ctx.wait(V, ro_free_sem[rpar], ro_free[rpar])
                    ev = ctx.op(V, lambda v, rpar=rpar, pp=pp: v.tensor_copy(
                        ro_sb[rpar][:], pso[pp][:]), "dve")
                    pso_free[pp] = ev
                    ctx.wait(Y, "dve", ev)
                    ro_free[rpar] = ctx.op(Y, lambda sy, tb=tb, oc=oc, rpar=rpar: sy.dma_start(
                        out=vt[tb * 128:(tb + 1) * 128, oc * 512:(oc + 1) * 512],
                        in_=ro_sb[rpar][:]), f"dst{rpar}", 16)
                    ro_free_sem[rpar] = f"dst{rpar}"
                else:
                    rpar = pp
                    ctx.wait(V, "pe", mm)
                    if ro_free_sem[rpar]:
                        ctx.wait(V, ro_free_sem[rpar], ro_free[rpar])
                    cc = cs_sb[:, tb, csel:csel + 1]
                    ss_ = cs_sb[:, tb, csel + 1:csel + 2]
                    ns_ = cs_sb[:, tb, csel + 2:csel + 3]
                    ctx.wait(V, "dc", consts)
                    ctx.op(V, lambda v, rpar=rpar, pp=pp, cc=cc: v.tensor_scalar(
                        out=ro_sb[rpar][:, 0:512:2], in0=pso[pp][:, 0:512:2],
                        scalar1=cc, scalar2=None, op0=ALU.mult), "dve")
                    ctx.op(V, lambda v, rpar=rpar, pp=pp, cc=cc: v.tensor_scalar(
                        out=ro_sb[rpar][:, 1:512:2], in0=pso[pp][:, 1:512:2],
                        scalar1=cc, scalar2=None, op0=ALU.mult), "dve")
                    ctx.wait(V, "dve")
                    ctx.op(V, lambda v, rpar=rpar, pp=pp, ss_=ss_: v.scalar_tensor_tensor(
                        out=ro_sb[rpar][:, 1:512:2], in0=pso[pp][:, 0:512:2],
                        scalar=ss_, in1=ro_sb[rpar][:, 1:512:2],
                        op0=ALU.mult, op1=ALU.add), "dve")
                    ctx.wait(V, "dve")
                    ev = ctx.op(V, lambda v, rpar=rpar, pp=pp, ns_=ns_: v.scalar_tensor_tensor(
                        out=ro_sb[rpar][:, 0:512:2], in0=pso[pp][:, 1:512:2],
                        scalar=ns_, in1=ro_sb[rpar][:, 0:512:2],
                        op0=ALU.mult, op1=ALU.add), "dve")
                    pso_free[pp] = ev
                    for hh in range(4):
                        tp = hh % 2
                        ctx.wait(T, "dve", max(ev, psT2_free[tp]))
                        pe_n = ctx.op(T, lambda t, rpar=rpar, hh=hh, tp=tp: t.transpose(
                            psT[tp][:], ro_sb[rpar][:, hh * 128:(hh + 1) * 128],
                            id_sb[:]), "pe")
                        ctx.wait(V, "pe", pe_n)
                        tpar = hh % 2
                        if tt_free[tpar]:
                            ctx.wait(V, f"dst{tpar}", tt_free[tpar])
                        psT2_free[tp] = ctx.op(V, lambda v, tp=tp, tpar=tpar: v.tensor_copy(
                            tt_sb[tpar][:], psT[tp][:]), "dve")
                        ctx.wait(Y, "dve", psT2_free[tp])
                        tt_free[tpar] = ctx.op(Y, lambda sy, dst=dst, oc=oc, hh=hh,
                                               tb=tb, tpar=tpar: sy.dma_start(
                            out=dst[0:128, oc * 4 + hh, tb * 128:(tb + 1) * 128],
                            in_=tt_sb[tpar][:]), f"dst{tpar}", 16)
                    ro_free[rpar] = ctx.c["pe"]
                    ro_free_sem[rpar] = "pe"
            sl += 1


def stage_l1attn_a_weights(inp):
    wq = np.asarray(inp["L1_attn_wq"], np.float32)
    wk = np.asarray(inp["L1_attn_wk"], np.float32)
    wv = np.asarray(inp["L1_attn_wv"], np.float32)
    return {
        "wq_s": pack_slabs(np.ascontiguousarray(wq.T), 512),
        "wk_s": pack_slabs(np.ascontiguousarray(wk.T), 512),
        "wv_s": pack_slabs(np.ascontiguousarray(wv.T), 512),
        "ident": np.eye(128, dtype=np.float32),
    }


def cs_cols_for(core):
    b, (a0, a1), (b0, b1) = chunk_ranges(core)
    pos = np.concatenate([np.arange(a0, a1), np.arange(b0, b1)]).astype(np.float32)
    c = np.cos(pos)
    s = np.sin(pos)
    sc = 1.0 / np.sqrt(128.0)
    return np.stack([c * sc, s * sc, -s * sc, c, s, -s], axis=1).astype(np.float32)


NKA, NKB = 12, 28


def build_l1attn_b_program():
    nc = bass.Bass("TRN2", target_bir_lowering=False)
    xin = nc.dram_tensor("xin", [TLOC, D], F32, kind="ExternalInput").ap()
    qt = nc.dram_tensor("qt", [128, 32, TLOC], BF16, kind="ExternalInput").ap()
    kta = nc.dram_tensor("kta", [128, 32, NKA * 128], BF16, kind="ExternalInput").ap()
    ktb = nc.dram_tensor("ktb", [128, 32, NKB * 128], BF16, kind="ExternalInput").ap()
    ktd = nc.dram_tensor("ktd", [128, 32, TLOC], BF16, kind="ExternalInput").ap()
    vta = nc.dram_tensor("vta", [32, 128, NKA * 129], BF16, kind="ExternalInput").ap()
    vtb = nc.dram_tensor("vtb", [32, 128, NKB * 129], BF16, kind="ExternalInput").ap()
    vtd = nc.dram_tensor("vtd", [32, 128, 8 * 129], BF16, kind="ExternalInput").ap()
    wos = nc.dram_tensor("wo_s", [8, 128, 32 * 512], F32, kind="ExternalInput").ap()
    ident = nc.dram_tensor("ident", [128, 128], F32, kind="ExternalInput").ap()
    trimask = nc.dram_tensor("trimask", [128, 128], F32, kind="ExternalInput").ap()
    xout = nc.dram_tensor("xout", [TLOC, D], F32, kind="ExternalOutput").ap()

    with ExitStack() as es:
        ctx = Ctx(nc, es)
        emit_l1attn_b(ctx, xin, qt, kta, ktb, ktd, vta, vtb, vtd, wos,
                      ident, trimask, xout)
        ctx.wait("sync", "dst0")
        ctx.wait("sync", "dst1")
        ctx.flush()
    return nc


def emit_l1attn_b(ctx, xin, qt, kta, ktb, ktd, vta, vtb, vtd, wos,
                  ident, trimask, xout):
    ctx.sems("dc", "kv0", "kv1", "dxs0", "dxs1", "dxs2", "dxs3",
             "dw0", "dw1", "pe", "act", "dve", "dst0", "dst1")
    T, V, S, G, Y = "tensor", "vector", "scalar", "gpsimd", "sync"

    id_sb = ctx.sb("b_id", [128, 128], BF16)
    tri_sb = ctx.sb("b_tri", [128, 128], BF16)
    qt_sb = ctx.sb("b_qt", [128, 32, TLOC], BF16)
    ot_sb = ctx.sb("b_ot", [128, 32, TLOC], BF16)

    ctx.wait(G, "dc")
    ctx.op(G, lambda g: g.dma_start(out=id_sb[:], in_=ident[:]), "dc", 16)
    ctx.wait(G, "dc")
    ctx.op(G, lambda g: g.dma_start(out=tri_sb[:], in_=trimask[:]), "dc", 16)
    consts = ctx.c["dc"]
    ctx.op(Y, lambda sy: sy.dma_start(out=qt_sb[:], in_=qt[:]), "dst0", 16)
    qt_ready = ("dst0", ctx.c["dst0"])

    ot_ready = [0] * NTB

    with ExitStack() as p1:
        c1 = sub(ctx, p1)
        ka = [c1.sb(f"b_ka{i}", [128, NKA * 128], BF16) for i in range(2)]
        kb_ = [c1.sb(f"b_kb{i}", [128, NKB * 128], BF16) for i in range(2)]
        kd = [c1.sb(f"b_kd{i}", [128, TLOC], BF16) for i in range(2)]
        va = [c1.sb(f"b_va{i}", [128, NKA * 129], BF16) for i in range(2)]
        vb = [c1.sb(f"b_vb{i}", [128, NKB * 129], BF16) for i in range(2)]
        vd = [c1.sb(f"b_vd{i}", [128, 8 * 129], BF16) for i in range(2)]
        probs = [c1.sb(f"b_pr{i}", [128, 512], BF16) for i in range(2)]
        prd = c1.sb("b_prd", [128, 128], BF16)
        rz = c1.sb("b_rz", [128, 1], F32)
        oh = c1.sb("b_oh", [128, 128], BF16)

        pss = [c1.ps(f"b_pss{i}", [128, 512]) for i in range(2)]
        pv = [c1.ps(f"b_pv{i}", [128, 129]) for i in range(4)]
        psd = c1.ps("b_psd", [128, 128])
        psT = c1.ps("b_psT", [128, 128], BF16)

        kv_ready = [None, None]
        kv_free = [0, 0]
        pss_free = [0, 0]
        probs_free = [0, 0]
        pv_free = [0, 0, 0, 0]
        psd_free = 0
        prd_free = 0
        rz_free = 0
        oh_free = 0
        psT_free = 0

        def load_head(h, par):
            for src, dst_t, semn in ((kta, ka, None), (ktb, kb_, None), (ktd, kd, None),
                                     (vta, va, None), (vtb, vb, None), (vtd, vd, None)):
                ctx.wait(Y, "pe", kv_free[par])
                if src in (kta, ktb, ktd):
                    ctx.op(Y, lambda sy, src=src, dst_t=dst_t, h=h, par=par: sy.dma_start(
                        out=dst_t[par][:], in_=src[:, h, :]), f"kv{par}", 16)
                else:
                    ctx.op(Y, lambda sy, src=src, dst_t=dst_t, h=h, par=par: sy.dma_start(
                        out=dst_t[par][:], in_=src[h]), f"kv{par}", 16)
            kv_ready[par] = (f"kv{par}", ctx.c[f"kv{par}"])

        load_head(0, 0)
        for h in range(32):
            par = h % 2
            if h + 1 < 32:
                load_head(h + 1, (h + 1) % 2)
            ctx.wait(T, *kv_ready[par])
            ctx.wait(T, *qt_ready)
            ctx.wait(T, "dc", consts)
            for chunk in range(2):
                nkb = (NKA, NKB)[chunk]
                ksrc = (ka, kb_)[chunk][par]
                vsrc = (va, vb)[chunk][par]
                qc0 = chunk * 512
                for kbi in range(nkb):
                    gp = kbi % 2
                    ctx.wait(T, "act", pss_free[gp])
                    ctx.op(T, lambda t, kbi=kbi, gp=gp, ksrc=ksrc, qc0=qc0, h=h:
                           t.matmul(pss[gp][:],
                                    ksrc[:, kbi * 128:(kbi + 1) * 128],
                                    qt_sb[:, h, qc0:qc0 + 512],
                                    start=True, stop=True), "pe")
                    smm = ctx.c["pe"]
                    ctx.wait(S, "pe", max(smm, probs_free[gp]))
                    e_n = ctx.op(S, lambda s, gp=gp: s.activation(
                        out=probs[gp][:], in_=pss[gp][:], func=AF.Exp), "act")
                    pss_free[gp] = e_n
                    ctx.wait(T, "act", e_n)
                    if kbi == 0:
                        ctx.wait(T, "act", max(pv_free[0], pv_free[1],
                                               pv_free[2], pv_free[3]))
                    for qb in range(4):
                        ctx.op(T, lambda t, kbi=kbi, qb=qb, gp=gp, vsrc=vsrc:
                               t.matmul(pv[qb][:],
                                        probs[gp][:, qb * 128:(qb + 1) * 128],
                                        vsrc[:, kbi * 129:(kbi + 1) * 129],
                                        start=(kbi == 0), stop=False),
                               "pe" if qb == 3 else None)
                    probs_free[gp] = ctx.c["pe"]

                for qb in range(4):
                    lq = chunk * 4 + qb
                    for j in range(qb + 1):
                        lk = chunk * 4 + j
                        ctx.wait(T, "act", psd_free)
                        ctx.op(T, lambda t, lk=lk, lq=lq, h=h, par=par: t.matmul(
                            psd[:], kd[par][:, lk * 128:(lk + 1) * 128],
                            qt_sb[:, h, lq * 128:(lq + 1) * 128],
                            start=True, stop=True), "pe")
                        ctx.wait(S, "pe", max(ctx.c["pe"], prd_free))
                        psd_free = ctx.op(S, lambda s: s.activation(
                            out=prd[:], in_=psd[:], func=AF.Exp), "act")
                        if j == qb:
                            ctx.wait(V, "act", psd_free)
                            m_n = ctx.op(V, lambda v: v.tensor_tensor(
                                out=prd[:], in0=prd[:], in1=tri_sb[:], op=ALU.mult), "dve")
                            ctx.wait(T, "dve", m_n)
                        else:
                            ctx.wait(T, "act", psd_free)
                        ctx.op(T, lambda t, lk=lk, lq=lq, qb=qb, par=par: t.matmul(
                            pv[qb][:], prd[:],
                            vd[par][:, lk * 129:(lk + 1) * 129],
                            start=False, stop=(lk == lq)), "pe")
                        prd_free = ctx.c["pe"]
                pv_mm = ctx.c["pe"]
                for qb in range(4):
                    ctx.wait(V, "pe", pv_mm)
                    ctx.wait(V, "act", rz_free)
                    r_n = ctx.op(V, lambda v, qb=qb: v.reciprocal(
                        rz[:], pv[qb][:, 128:129]), "dve")
                    ctx.wait(S, "dve", r_n)
                    ctx.wait(S, "pe", oh_free)
                    ev_n = ctx.op(S, lambda s, qb=qb: s.activation(
                        out=oh[:], in_=pv[qb][:, 0:128],
                        func=AF.Copy, scale=rz[:]), "act")
                    rz_free = ev_n
                    pv_free[qb] = ev_n
                    ctx.wait(T, "act", ev_n)
                    ctx.wait(T, "dve", psT_free)
                    tp_n = ctx.op(T, lambda t: t.transpose(psT[:], oh[:], id_sb[:]), "pe")
                    oh_free = tp_n
                    ctx.wait(V, "pe", tp_n)
                    psT_free = ctx.op(V, lambda v, h=h, qc0=qc0, qb=qb: v.tensor_copy(
                        ot_sb[:, h, qc0 + qb * 128:qc0 + (qb + 1) * 128], psT[:]), "dve")
            kv_free[par] = ctx.c["pe"]
        p1_pe = ctx.c["pe"]
        p1_dve = ctx.c["dve"]
        p1_act = ctx.c["act"]
        ot_done = ctx.c["dve"]

    with ExitStack() as p2:
        c2 = sub(ctx, p2)
        wsl = [c2.sb(f"b_wsl{i}", [128, 32 * 512], BF16) for i in range(2)]
        xsl = [c2.sb(f"b_xsl{i}", [128, 512], F32) for i in range(4)]
        xo_sb = [c2.sb(f"b_xo{i}", [128, 512], F32) for i in range(2)]
        psp = [c2.ps(f"b_psp{i}", [128, 512]) for i in range(2)]

        wsl_ready = [None, None]
        wsl_free = [0, 0]
        ctx.wait(G, "pe", p1_pe)
        ctx.wait(G, "dve", p1_dve)
        ctx.wait(G, "act", p1_act)
        ctx.wait(T, "act", p1_act)
        ctx.op(G, lambda g: g.dma_start(out=wsl[0][:], in_=wos[0]), "dw0", 16)
        wsl_ready[0] = ("dw0", ctx.c["dw0"])

        psp_free = [0, 0]
        xsl_free = [0] * 4
        xo_free = [0, 0]
        it = 0
        for oc in range(8):
            wp = oc % 2
            if oc + 1 < 8:
                npar = (oc + 1) % 2
                ctx.wait(G, "pe", wsl_free[npar])
                ctx.op(G, lambda g, oc=oc, npar=npar: g.dma_start(
                    out=wsl[npar][:], in_=wos[oc + 1]), f"dw{npar}", 16)
                wsl_ready[npar] = (f"dw{npar}", ctx.c[f"dw{npar}"])
            for tb in range(NTB):
                par = it % 2
                xp = it % 4
                ctx.wait(Y, "dve", xsl_free[xp])
                ctx.op(Y, lambda sy, tb=tb, oc=oc, xp=xp: sy.dma_start(
                    out=xsl[xp][:], in_=xin[tb * 128:(tb + 1) * 128, oc * 512:(oc + 1) * 512]),
                    f"dxs{xp}", 16)
                xsl_n = (f"dxs{xp}", ctx.c[f"dxs{xp}"])

                ctx.wait(T, *wsl_ready[wp])
                ctx.wait(T, "dve", max(ot_done, psp_free[par]))
                for k in range(32):
                    mm_n = ctx.op(T, lambda t, tb=tb, k=k, par=par, wp=wp: t.matmul(
                        psp[par][:], ot_sb[:, k, tb * 128:(tb + 1) * 128],
                        wsl[wp][:, k * 512:(k + 1) * 512],
                        start=(k == 0), stop=(k == 31)), "pe" if k == 31 else None)
                if tb == NTB - 1:
                    wsl_free[wp] = mm_n
                ctx.wait(V, "pe", mm_n)
                ctx.wait(V, *xsl_n)
                ctx.wait(V, f"dst{par}", xo_free[par])
                ep_n = ctx.op(V, lambda v, par=par, xp=xp: v.tensor_tensor(
                    out=xo_sb[par][:], in0=psp[par][:], in1=xsl[xp][:],
                    op=ALU.add), "dve")
                psp_free[par] = ep_n
                xsl_free[xp] = ep_n
                ctx.wait(Y, "dve", ep_n)
                xo_free[par] = ctx.op(Y, lambda sy, tb=tb, oc=oc, par=par: sy.dma_start(
                    out=xout[tb * 128:(tb + 1) * 128, oc * 512:(oc + 1) * 512],
                    in_=xo_sb[par][:]), f"dst{par}", 16)
                it += 1


def stage_l1attn_b(inp, s5a_outs):
    import ml_dtypes
    bf = ml_dtypes.bfloat16
    ONE = np.float32(1.0).astype(bf).view(np.uint16)
    wo = np.asarray(inp["L1_attn_wo"], np.float32)
    k_idx, q_idx = np.meshgrid(np.arange(128), np.arange(128), indexing="ij")
    shared = {
        "wo_s": pack_slabs(np.ascontiguousarray(wo.T), 512),
        "ident": np.eye(128, dtype=np.float32),
        "trimask": (k_idx <= q_idx).astype(np.float32),
    }
    percore = [None] * 8
    for bb in range(2):
        cores = [bb * 4 + c for c in range(4)]
        ktg = np.zeros((128, 32, 4096), np.uint16)
        vg = np.zeros((4096, 32, 129), np.uint16)
        for c in range(4):
            co = cores[c]
            kt_c = np.asarray(s5a_outs[co]["kt"]).view(np.uint16)
            vt_c = np.asarray(s5a_outs[co]["vt"]).view(np.uint16).reshape(1024, 32, 128)
            for half, u in ((0, c), (1, 7 - c)):
                ktg[:, :, u * 512:(u + 1) * 512] = kt_c[:, :, half * 512:(half + 1) * 512]
                vg[u * 512:(u + 1) * 512, :, :128] = vt_c[half * 512:(half + 1) * 512]
                vg[u * 512:(u + 1) * 512, :, 128] = ONE
        for c in range(4):
            co = cores[c]
            kta = ktg[:, :, :NKA * 128].copy()
            kta[:, :, 512 * c:] = 0
            ktb = ktg[:, :, :NKB * 128].copy()
            ktb[:, :, 512 * (7 - c):] = 0
            va = vg[:NKA * 128].copy()
            va[512 * c:] = 0
            vb = vg[:NKB * 128].copy()
            vb[512 * (7 - c):] = 0
            vd = np.zeros((1024, 32, 129), np.uint16)
            vd[:, :, :128] = np.asarray(s5a_outs[co]["vt"]).view(np.uint16).reshape(1024, 32, 128)
            vd[:, :, 128] = ONE

            def vhead(vv, nkb):
                a = vv.reshape(nkb, 128, 32, 129).transpose(2, 1, 0, 3)
                return np.ascontiguousarray(a.reshape(32, 128, nkb * 129)).view(bf)

            percore[co] = {
                "qt": np.ascontiguousarray(np.asarray(s5a_outs[co]["qt"]).view(np.uint16)).view(bf),
                "kta": np.ascontiguousarray(kta).view(bf),
                "ktb": np.ascontiguousarray(ktb).view(bf),
                "ktd": np.ascontiguousarray(np.asarray(s5a_outs[co]["kt"]).view(np.uint16)).view(bf),
                "vta": vhead(va, NKA),
                "vtb": vhead(vb, NKB),
                "vtd": vhead(vd, 8),
            }
    return shared, percore


_PROGS = {}


def _prog(name, builder):
    if name not in _PROGS:
        _PROGS[name] = builder()
    return _PROGS[name]


def _run(nc, in_maps):
    from concourse.bass_utils import run_bass_kernel_spmd
    return run_bass_kernel_spmd(nc, in_maps, core_ids=list(range(NCORES)))


def kernel(**inputs):
    x = np.ascontiguousarray(np.asarray(inputs["L0_attn_inputs"], np.float32))

    nc1 = _prog("l0ffn", build_l0ffn_program)
    w1 = stage_l0ffn_weights(inputs)
    res = _run(nc1, [dict(w1, xin=slice_owned(x, c)) for c in range(NCORES)])
    x1 = np.empty_like(x)
    unslice_owned([res.results[c]["xout"] for c in range(NCORES)], x1)

    nc2 = _prog("l0attn", build_l0attn_program)
    w2 = stage_l0attn_weights(inputs)
    res = _run(nc2, [dict(w2, xin_h=slice_haloed(x1, c)) for c in range(NCORES)])
    x2 = np.empty_like(x)
    unslice_owned([res.results[c]["xout"] for c in range(NCORES)], x2)

    res = _run(nc1, [dict(w1, xin=slice_owned(x2, c)) for c in range(NCORES)])
    x3 = np.empty_like(x)
    unslice_owned([res.results[c]["xout"] for c in range(NCORES)], x3)

    nc4 = _prog("l1ffn", build_l1ffn_program)
    w4 = stage_l1ffn_weights(inputs)
    res = _run(nc4, [dict(w4, xin=slice_owned(x3, c)) for c in range(NCORES)])
    x4 = np.empty_like(x)
    unslice_owned([res.results[c]["xout"] for c in range(NCORES)], x4)

    nc5a = _prog("l1attn_a", build_l1attn_a_program)
    w5a = stage_l1attn_a_weights(inputs)
    res = _run(nc5a, [dict(w5a, xin=slice_owned(x4, c), cs_cols=cs_cols_for(c))
                      for c in range(NCORES)])
    s5a_outs = [res.results[c] for c in range(NCORES)]
    nc5b = _prog("l1attn_b", build_l1attn_b_program)
    shared, percore = stage_l1attn_b(inputs, s5a_outs)
    res = _run(nc5b, [dict(shared, **percore[c], xin=slice_owned(x4, c))
                      for c in range(NCORES)])
    x5 = np.empty_like(x)
    unslice_owned([res.results[c]["xout"] for c in range(NCORES)], x5)

    res = _run(nc4, [dict(w4, xin=slice_owned(x5, c)) for c in range(NCORES)])
    x6 = np.empty_like(x)
    unslice_owned([res.results[c]["xout"] for c in range(NCORES)], x6)
    return x6


# version 02
# speedup vs baseline: 1.0052x; 1.0052x over previous
from contextlib import ExitStack

import numpy as np

import concourse.bass as bass
import concourse.mybir as mybir

F32 = mybir.dt.float32
BF16 = mybir.dt.bfloat16
AF = mybir.ActivationFunctionType
ALU = mybir.AluOpType
AX = mybir.AxisListType

D = 4096
NCORES = 8
TLOC = 1024
NTB = TLOC // 128
LN_EPS = 1e-5

ENGINES = ("tensor", "vector", "scalar", "gpsimd", "sync")


def chunk_ranges(core):
    b, c = divmod(core, 4)
    a0 = 512 * c
    b0 = 512 * (7 - c)
    return b, (a0, a0 + 512), (b0, b0 + 512)


def slice_owned(x_full, core):
    b, (a0, a1), (b0, b1) = chunk_ranges(core)
    return np.ascontiguousarray(
        np.concatenate([x_full[b, a0:a1], x_full[b, b0:b1]], axis=0))


def unslice_owned(shards, out):
    for core in range(NCORES):
        b, (a0, a1), (b0, b1) = chunk_ranges(core)
        out[b, a0:a1] = shards[core][:512]
        out[b, b0:b1] = shards[core][512:]
    return out


def pack_slabs(w_in_out, slab_cols):
    IN, OUT = w_in_out.shape
    nk = IN // 128
    ns = OUT // slab_cols
    w = w_in_out.reshape(nk, 128, ns, slab_cols).transpose(2, 1, 0, 3)
    return np.ascontiguousarray(w.reshape(ns, 128, nk * slab_cols))


class Ctx:

    def __init__(self, nc, es):
        self.nc = nc
        self.es = es
        self.s = {}
        self.c = {}
        self.q = {e: [] for e in ENGINES}

    def sems(self, *names):
        for n in names:
            if n not in self.s:
                self.s[n] = self.es.enter_context(self.nc.semaphore(n))
                self.c[n] = 0

    def sb(self, name, shape, dtype):
        return self.es.enter_context(self.nc.sbuf_tensor(name, shape, dtype))

    def ps(self, name, shape, dtype=F32):
        return self.es.enter_context(self.nc.psum_tensor(name, shape, dtype))

    def wait(self, eng, sem, val=None):
        if val is None:
            val = self.c[sem]
        if val <= 0:
            return
        h = self.s[sem]
        self.q[eng].append(lambda e: e.wait_ge(h, val))

    def op(self, eng, fn, sem=None, n=1):
        if sem is None:
            self.q[eng].append(fn)
            return None
        h = self.s[sem]
        self.c[sem] += n
        self.q[eng].append(lambda e: fn(e).then_inc(h, n))
        return self.c[sem]

    def flush(self):
        with self.nc.Block() as block:
            for name in ENGINES:
                thunks = self.q[name]
                if not thunks:
                    continue
                getattr(block, name)(lambda e, th=tuple(thunks): [f(e) for f in th])
        self.q = {e: [] for e in ENGINES}


def alloc_stats(ctx, tag, sq=None):
    return {
        "sq": sq if sq is not None else ctx.sb(f"{tag}_sq", [128, D], BF16),
        "ss": ctx.sb(f"{tag}_ss", [128, 1], F32),
        "sm": ctx.sb(f"{tag}_sm", [128, 1], F32),
        "mu": ctx.sb(f"{tag}_mu", [128, 1], F32),
        "var": ctx.sb(f"{tag}_var", [128, 1], F32),
        "rvar": ctx.sb(f"{tag}_rvar", [128, 1], F32),
        "rstd": ctx.sb(f"{tag}_rstd", [128, 1], F32),
        "nmr": ctx.sb(f"{tag}_nmr", [128, 1], F32),
    }


def emit_ln_stats(ctx, xb, st, sq=None, ready=None):
    V, S = "vector", "scalar"
    sq = st["sq"] if sq is None else sq
    if ready is not None:
        ctx.wait(S, *ready)
        ctx.wait(V, *ready)
    ss_n = ctx.op(S, lambda s: s.activation(out=sq[:], in_=xb[:], func=AF.Square,
                                            accum_out=st["ss"][:]), "act")
    ctx.op(V, lambda v: v.reduce_sum(st["sm"][:], xb[:], axis=AX.X), "dve")
    ctx.wait(V, "dve")
    ctx.op(V, lambda v: v.tensor_scalar(out=st["mu"][:], in0=st["sm"][:],
                                        scalar1=1.0 / D, scalar2=None, op0=ALU.mult), "dve")
    ctx.wait(V, "dve")
    ctx.op(V, lambda v: v.tensor_tensor(out=st["var"][:], in0=st["mu"][:],
                                        in1=st["mu"][:], op=ALU.mult), "dve")
    ctx.wait(V, "dve")
    ctx.wait(V, "act", ss_n)
    ctx.op(V, lambda v: v.scalar_tensor_tensor(
        out=st["var"][:], in0=st["ss"][:], scalar=1.0 / D, in1=st["var"][:],
        op0=ALU.mult, op1=ALU.subtract), "dve")
    ctx.wait(V, "dve")
    ctx.op(V, lambda v: v.tensor_scalar(out=st["var"][:], in0=st["var"][:],
                                        scalar1=LN_EPS, scalar2=None, op0=ALU.add), "dve")
    ctx.wait(V, "dve")
    ctx.op(V, lambda v: v.reciprocal(st["rvar"][:], st["var"][:]), "dve")
    ctx.wait(S, "dve")
    ctx.op(S, lambda s: s.activation(out=st["rstd"][:], in_=st["rvar"][:],
                                     func=AF.Sqrt), "act")
    ctx.wait(V, "act")
    return ctx.op(V, lambda v: v.scalar_tensor_tensor(
        out=st["nmr"][:], in0=st["mu"][:], scalar=-1.0, in1=st["rstd"][:],
        op0=ALU.mult, op1=ALU.mult), "dve")


def build_l0ffn_program():
    nc = bass.Bass("TRN2", target_bir_lowering=False)
    xin = nc.dram_tensor("xin", [TLOC, D], F32, kind="ExternalInput").ap()
    wup = nc.dram_tensor("wup_s", [8, 128, 32 * 512], F32, kind="ExternalInput").ap()
    gate_t = nc.dram_tensor("gate_t", [D, 4], F32, kind="ExternalInput").ap()
    ebias = nc.dram_tensor("ebias", [1, 4], F32, kind="ExternalInput").ap()
    lora_at = nc.dram_tensor("lora_at", [D, 32], F32, kind="ExternalInput").ap()
    lora_bs = nc.dram_tensor("lora_bs", [32, D], F32, kind="ExternalInput").ap()
    t8 = nc.dram_tensor("t8", [4, 32], F32, kind="ExternalInput").ap()
    ident = nc.dram_tensor("ident", [128, 128], F32, kind="ExternalInput").ap()
    xout = nc.dram_tensor("xout", [TLOC, D], F32, kind="ExternalOutput").ap()

    with ExitStack() as es:
        ctx = Ctx(nc, es)
        emit_l0ffn(ctx, xin, xout, wup, gate_t, ebias, lora_at, lora_bs, t8, ident)
        ctx.wait("sync", "dst0")
        ctx.wait("sync", "dst1")
        ctx.flush()
    return nc


def emit_l0ffn(ctx, xin, xout, wup, gate_t, ebias, lora_at, lora_bs, t8, ident):
    ctx.sems("dc", "dx0", "dx1", "dxs0", "dxs1", "dxs2", "dxs3", "dw0", "dw1",
             "pe", "act", "dve", "dst0", "dst1")
    T, V, S, G, Y = "tensor", "vector", "scalar", "gpsimd", "sync"

    gate_sb = ctx.sb("gate_sb", [128, 32, 4], BF16)
    ebias_sb = ctx.sb("ebias_sb", [128, 4], F32)
    la_sb = ctx.sb("la_sb", [128, 32, 32], BF16)
    lb_sb = ctx.sb("lb_sb", [32, D], BF16)
    t8_sb = ctx.sb("t8_sb", [4, 32], BF16)
    id_sb = ctx.sb("id_sb", [128, 128], BF16)
    xb = [ctx.sb(f"xb{i}", [128, D], F32) for i in range(2)]
    xnb = [ctx.sb(f"xnb{i}", [128, D], BF16) for i in range(2)]
    xnt = ctx.sb("xnt", [128, NTB * D], BF16)
    wsl = [ctx.sb(f"wsl{i}", [128, 32 * 512], BF16) for i in range(2)]
    zsel = ctx.sb("zsel", [32, NTB * 128], BF16)
    logits = ctx.sb("logits", [128, 4], F32)
    gmax = ctx.sb("gmax", [128, 1], F32)
    onehot = ctx.sb("onehot", [128, 4], BF16)
    onehT = ctx.sb("onehT", [4, 128], BF16)
    mask_sb = ctx.sb("mask_sb", [32, 128], BF16)
    xsl = [ctx.sb(f"xsl{i}", [128, 512], F32) for i in range(4)]
    xo_sb = [ctx.sb(f"xo{i}", [128, 512], F32) for i in range(2)]
    st = alloc_stats(ctx, "st", sq=xnb[0])

    psT = [ctx.ps(f"psT{i}", [128, 128], BF16) for i in range(2)]
    psg = ctx.ps("psg", [128, 4])
    psoh = ctx.ps("psoh", [4, 128], BF16)
    psm = ctx.ps("psm", [32, 128])
    psz = ctx.ps("psz", [32, 128])
    pso = [ctx.ps(f"pso{i}", [128, 512]) for i in range(2)]

    for fn in (
        lambda g: g.dma_start(out=gate_sb[:], in_=gate_t.rearrange("(k p) e -> p k e", p=128)),
        lambda g: g.dma_start(out=la_sb[:], in_=lora_at.rearrange("(k p) e -> p k e", p=128)),
        lambda g: g.dma_start(out=lb_sb[:], in_=lora_bs[:]),
        lambda g: g.dma_start(out=t8_sb[:], in_=t8[:]),
        lambda g: g.dma_start(out=id_sb[:], in_=ident[:]),
        lambda g: g.dma_start(out=ebias_sb[:], in_=ebias.broadcast_to([128, 4])),
    ):
        ctx.wait(G, "dc")
        ctx.op(G, fn, "dc", 16)
    consts = ctx.c["dc"]

    wsl_ready = [None, None]
    wsl_free = [0, 0]
    ctx.wait(G, "dc", consts)
    ctx.op(G, lambda g: g.dma_start(out=wsl[0][:], in_=wup[0]), "dw0", 16)
    wsl_ready[0] = ("dw0", ctx.c["dw0"])

    xb_ready = [None] * NTB
    ctx.op(Y, lambda sy: sy.dma_start(out=xb[0][:], in_=xin[0:128, :]), "dx0", 16)
    xb_ready[0] = ("dx0", ctx.c["dx0"])
    ctx.op(Y, lambda sy: sy.dma_start(out=xb[1][:], in_=xin[128:256, :]), "dx1", 16)
    xb_ready[1] = ("dx1", ctx.c["dx1"])

    tp_done = [0] * NTB
    gate_done = [0] * NTB
    xnb_free = [0, 0]
    psT_free = [0, 0]
    psg_free = 0
    psz_free = 0
    psoh_free = 0
    psm_free = 0
    onehot_free = 0
    onehT_free = 0
    mask_free = 0

    for tb in range(NTB):
        par = tb % 2
        r0 = tb * 128

        if tb >= 2:
            ctx.wait("scalar", "pe", xnb_free[par])
        emit_ln_stats(ctx, xb[par], st, sq=xnb[par], ready=xb_ready[tb])
        ctx.wait(V, "dve")
        if tb >= 2:
            ctx.wait(V, "pe", xnb_free[par])
        ln_apply = ctx.op(V, lambda v, par=par: v.tensor_scalar(
            out=xnb[par][:], in0=xb[par][:], scalar1=st["rstd"][:],
            scalar2=st["nmr"][:], op0=ALU.mult, op1=ALU.add), "dve")

        if tb + 2 < NTB:
            ctx.wait(Y, "dve", ln_apply)
            ctx.op(Y, lambda sy, tb=tb, par=par: sy.dma_start(
                out=xb[par][:], in_=xin[(tb + 2) * 128:(tb + 3) * 128, :]), f"dx{par}", 16)
            xb_ready[tb + 2] = (f"dx{par}", ctx.c[f"dx{par}"])

        ctx.wait(T, "dc", consts)
        ctx.wait(T, "dve", ln_apply)
        for f in range(32):
            tpar = f % 2
            ctx.wait(T, "dve", psT_free[tpar])
            pe_n = ctx.op(T, lambda t, par=par, tpar=tpar, f=f: t.transpose(
                psT[tpar][:], xnb[par][:, f * 128:(f + 1) * 128], id_sb[:]), "pe")
            ctx.wait(V, "pe", pe_n)
            psT_free[tpar] = ctx.op(V, lambda v, tb=tb, tpar=tpar, f=f: v.tensor_copy(
                xnt[:, tb * D + f * 128: tb * D + (f + 1) * 128], psT[tpar][:]), "dve")
        xnb_free[par] = ctx.c["pe"]
        tp_done[tb] = ctx.c["dve"]

        ctx.wait(T, "dve", max(psg_free, tp_done[tb]))
        for k in range(32):
            ctx.op(T, lambda t, tb=tb, k=k: t.matmul(
                psg[:], xnt[:, tb * D + k * 128:tb * D + (k + 1) * 128],
                gate_sb[:, k, :],
                start=(k == 0), stop=(k == 31)), "pe" if k == 31 else None)
        ctx.wait(V, "pe")
        ctx.wait(V, "dc", consts)
        psg_free = ctx.op(V, lambda v: v.tensor_tensor(out=logits[:], in0=psg[:],
                                            in1=ebias_sb[:], op=ALU.add), "dve")
        ctx.wait(V, "dve")
        ctx.op(V, lambda v: v.reduce_max(gmax[:], logits[:], axis=AX.X), "dve")
        ctx.wait(V, "dve")
        ctx.wait(V, "pe", onehot_free)
        oh_n = ctx.op(V, lambda v: v.tensor_scalar(
            out=onehot[:], in0=logits[:], scalar1=gmax[:], scalar2=None,
            op0=ALU.is_equal), "dve")
        ctx.wait(T, "dve", max(oh_n, psoh_free))
        ohT_pe = ctx.op(T, lambda t: t.transpose(psoh[:], onehot[:], id_sb[:]), "pe")
        onehot_free = ohT_pe
        ctx.wait(V, "pe", max(ohT_pe, onehT_free))
        ohT_n = ctx.op(V, lambda v: v.tensor_copy(onehT[:], psoh[:]), "dve")
        psoh_free = ohT_n
        ctx.wait(T, "dve", ohT_n)
        ctx.wait(T, "act", psm_free)
        m_pe = ctx.op(T, lambda t: t.matmul(psm[:], t8_sb[:], onehT[:],
                                            start=True, stop=True), "pe")
        onehT_free = m_pe
        ctx.wait(S, "pe", m_pe)
        ctx.wait(S, "dve", mask_free)
        msk_n = ctx.op(S, lambda s: s.activation(out=mask_sb[:], in_=psm[:],
                                                 func=AF.Copy), "act")
        psm_free = msk_n
        ctx.wait(T, "dve", psz_free)
        for k in range(32):
            ctx.op(T, lambda t, tb=tb, k=k: t.matmul(
                psz[:], la_sb[:, k, :],
                xnt[:, tb * D + k * 128:tb * D + (k + 1) * 128],
                start=(k == 0), stop=(k == 31)), "pe" if k == 31 else None)
        ctx.wait(V, "pe")
        ctx.wait(V, "act", msk_n)
        gate_done[tb] = ctx.op(V, lambda v, tb=tb: v.tensor_tensor(
            out=zsel[:, tb * 128:(tb + 1) * 128], in0=psz[:], in1=mask_sb[:],
            op=ALU.mult), "dve")
        psz_free = gate_done[tb]
        mask_free = gate_done[tb]

    pso_free = [0, 0]
    xsl_free = [0] * 4
    xo_free = [0, 0]
    it = 0
    for oc in range(8):
        wpar = oc % 2
        if oc + 1 < 8:
            npar = (oc + 1) % 2
            ctx.wait(G, "pe", wsl_free[npar])
            ctx.op(G, lambda g, oc=oc, npar=npar: g.dma_start(
                out=wsl[npar][:], in_=wup[oc + 1]), f"dw{npar}", 16)
            wsl_ready[npar] = (f"dw{npar}", ctx.c[f"dw{npar}"])

        for tb in range(NTB):
            par = it % 2
            xp = it % 4
            ctx.wait(Y, "dve", xsl_free[xp])
            ctx.op(Y, lambda sy, tb=tb, oc=oc, xp=xp: sy.dma_start(
                out=xsl[xp][:], in_=xin[tb * 128:(tb + 1) * 128, oc * 512:(oc + 1) * 512]),
                f"dxs{xp}", 16)
            xsl_n = (f"dxs{xp}", ctx.c[f"dxs{xp}"])

            ctx.wait(T, *wsl_ready[wpar])
            ctx.wait(T, "dve", max(tp_done[tb], pso_free[par]))
            for k in range(32):
                ctx.op(T, lambda t, tb=tb, k=k, par=par, wpar=wpar: t.matmul(
                    pso[par][:], xnt[:, tb * D + k * 128:tb * D + (k + 1) * 128],
                    wsl[wpar][:, k * 512:(k + 1) * 512],
                    start=(k == 0), stop=False))
            ctx.wait(T, "dve", gate_done[tb])
            mm_n = ctx.op(T, lambda t, tb=tb, oc=oc, par=par: t.matmul(
                pso[par][:], zsel[:, tb * 128:(tb + 1) * 128],
                lb_sb[0:32, oc * 512:(oc + 1) * 512],
                start=False, stop=True), "pe")
            if tb == NTB - 1:
                wsl_free[wpar] = mm_n

            ctx.wait(V, "pe", mm_n)
            ctx.wait(V, *xsl_n)
            ctx.wait(V, f"dst{par}", xo_free[par])
            ep_n = ctx.op(V, lambda v, par=par, xp=xp: v.scalar_tensor_tensor(
                out=xo_sb[par][:], in0=pso[par][:], scalar=0.5, in1=xsl[xp][:],
                op0=ALU.mult, op1=ALU.add), "dve")
            pso_free[par] = ep_n
            xsl_free[xp] = ep_n
            ctx.wait(Y, "dve", ep_n)
            xo_free[par] = ctx.op(Y, lambda sy, tb=tb, oc=oc, par=par: sy.dma_start(
                out=xout[tb * 128:(tb + 1) * 128, oc * 512:(oc + 1) * 512],
                in_=xo_sb[par][:]), f"dst{par}", 16)
            it += 1


def stage_l0ffn_weights(inp):
    W_up = np.asarray(inp["L0_ffn_W_up"], np.float32)
    gate = np.asarray(inp["L0_ffn_gate_w"], np.float32)
    eb = np.asarray(inp["L0_ffn_expert_bias"], np.float32)
    A = np.asarray(inp["L0_ffn_lora_A"], np.float32)
    B = np.asarray(inp["L0_ffn_lora_B"], np.float32)
    t8 = np.zeros((4, 32), np.float32)
    for e in range(4):
        t8[e, e * 8:(e + 1) * 8] = 1.0
    return {
        "wup_s": pack_slabs(np.ascontiguousarray(W_up.T), 512),
        "gate_t": np.ascontiguousarray(gate.T),
        "ebias": eb.reshape(1, 4),
        "lora_at": np.ascontiguousarray(A.transpose(2, 0, 1).reshape(D, 32)),
        "lora_bs": np.ascontiguousarray(B.transpose(0, 2, 1).reshape(32, D)),
        "t8": t8,
        "ident": np.eye(128, dtype=np.float32),
    }



TIN = 1028


def xin_row(owned_idx):
    return 1 + owned_idx if owned_idx < 512 else 3 + owned_idx


def sub(ctx, es):
    c = Ctx.__new__(Ctx)
    c.__dict__.update(ctx.__dict__)
    c.es = es
    return c


def build_l0attn_program():
    nc = bass.Bass("TRN2", target_bir_lowering=False)
    xin = nc.dram_tensor("xin_h", [TIN, D], F32, kind="ExternalInput").ap()
    convw = nc.dram_tensor("convw_s", [3, 6, 128, 32 * 512], F32, kind="ExternalInput").ap()
    convb = nc.dram_tensor("convb", [1, 3072], F32, kind="ExternalInput").ap()
    wq = nc.dram_tensor("wq_r", [1024, 1024], F32, kind="ExternalInput").ap()
    wk = nc.dram_tensor("wk_r", [1024, 1024], F32, kind="ExternalInput").ap()
    wv = nc.dram_tensor("wv_r", [1024, 1024], F32, kind="ExternalInput").ap()
    wo = nc.dram_tensor("wo_r", [1024, 1024], F32, kind="ExternalInput").ap()
    bqc = nc.dram_tensor("bq_c", [128, 8], F32, kind="ExternalInput").ap()
    bkc = nc.dram_tensor("bk_c", [128, 8], F32, kind="ExternalInput").ap()
    bvr = nc.dram_tensor("bv_r", [1, 1024], F32, kind="ExternalInput").ap()
    bor = nc.dram_tensor("bo_r", [1, 1024], F32, kind="ExternalInput").ap()
    projw = nc.dram_tensor("projw_s", [8, 128, 8 * 512], F32, kind="ExternalInput").ap()
    projb = nc.dram_tensor("projb", [1, D], F32, kind="ExternalInput").ap()
    ident = nc.dram_tensor("ident", [128, 128], F32, kind="ExternalInput").ap()
    trimask = nc.dram_tensor("trimask", [128, 128], F32, kind="ExternalInput").ap()
    xout = nc.dram_tensor("xout", [TLOC, D], F32, kind="ExternalOutput").ap()

    with ExitStack() as es:
        ctx = Ctx(nc, es)
        emit_l0attn(ctx, xin, xout, convw, convb, wq, wk, wv, wo, bqc, bkc,
                    bvr, bor, projw, projb, ident, trimask)
        ctx.wait("sync", "dst0")
        ctx.wait("sync", "dst1")
        ctx.flush()
    return nc


def emit_l0attn(ctx, xin, xout, convw, convb, wq, wk, wv, wo, bqc, bkc,
                bvr, bor, projw, projb, ident, trimask):
    ctx.sems("dc", "dx0", "dx1", "dxs0", "dxs1", "dxs2", "dxs3",
             "dw0", "dw1", "pe", "act", "dve", "dst0", "dst1")
    T, V, S, G, Y = "tensor", "vector", "scalar", "gpsimd", "sync"

    id_sb = ctx.sb("id_sb", [128, 128], BF16)
    tri_sb = ctx.sb("tri_sb", [128, 128], BF16)
    ones_sb = ctx.sb("ones_sb", [1, 128], BF16)
    convb_sb = ctx.sb("convb_sb", [1, 3072], BF16)
    bq_sb = ctx.sb("bq_sb", [128, 8], F32)
    bk_sb = ctx.sb("bk_sb", [128, 8], F32)
    bv_sb = ctx.sb("bv_sb", [1, 1024], BF16)
    bo_sb = ctx.sb("bo_sb", [1, 1024], BF16)
    projb_sb = ctx.sb("projb_sb", [1, D], BF16)
    qkvc = ctx.sb("qkvc", [128, NTB, 3072], BF16)

    for fn in (
        lambda g: g.dma_start(out=id_sb[:], in_=ident[:]),
        lambda g: g.dma_start(out=tri_sb[:], in_=trimask[:]),
        lambda g: g.dma_start(out=convb_sb[:], in_=convb[:]),
        lambda g: g.dma_start(out=bv_sb[:], in_=bvr[:]),
        lambda g: g.dma_start(out=bo_sb[:], in_=bor[:]),
        lambda g: g.dma_start(out=projb_sb[:], in_=projb[:]),
    ):
        ctx.wait(G, "dc")
        ctx.op(G, fn, "dc", 16)
    ctx.wait(G, "dc")
    ctx.op(G, lambda g: g.dma_start(out=bq_sb[:], in_=bqc[:]), "dc", 16)
    ctx.wait(G, "dc")
    ctx.op(G, lambda g: g.dma_start(out=bk_sb[:], in_=bkc[:]), "dc", 16)
    consts = ctx.c["dc"]
    ctx.op(V, lambda v: v.memset(ones_sb[:], 1.0), "dve")
    ctx.wait(V, "dc", consts)
    ctx.op(V, lambda v: v.tensor_scalar(out=bq_sb[:], in0=bq_sb[:], scalar1=0.125,
                                        scalar2=None, op0=ALU.mult), "dve")
    bq_ready = ctx.c["dve"]
    ones_ready = ctx.c["dve"]

    qkvc_ready = [0] * NTB
    ab0 = [xin_row(128 * i) for i in range(NTB)]

    with ExitStack() as p12:
        c12 = sub(ctx, p12)
        xnt = c12.sb("xnt2", [128, 32, TIN], BF16)

        with ExitStack() as p1:
            c1 = sub(ctx, p1)
            xb = [c1.sb(f"a_xb{i}", [128, D], F32) for i in range(2)]
            xnb = [c1.sb(f"a_xnb{i}", [128, D], BF16) for i in range(2)]
            st = alloc_stats(c1, "a_st", sq=xnb[0])
            psT = [c1.ps(f"a_psT{i}", [128, 128], BF16) for i in range(2)]

            t0s = [min(i * 128, TIN - 128) for i in range(9)]
            xb_ready = [None] * 9
            for i in range(2):
                ctx.op(Y, lambda sy, i=i: sy.dma_start(
                    out=xb[i][:], in_=xin[t0s[i]:t0s[i] + 128, :]), f"dx{i % 2}", 16)
                xb_ready[i] = (f"dx{i % 2}", ctx.c[f"dx{i % 2}"])

            xnb_free = [0, 0]
            psT_free = [0, 0]
            for i in range(9):
                par = i % 2
                if i >= 2:
                    ctx.wait(S, "pe", xnb_free[par])
                emit_ln_stats(ctx, xb[par], st, sq=xnb[par], ready=xb_ready[i])
                ctx.wait(V, "dve")
                if i >= 2:
                    ctx.wait(V, "pe", xnb_free[par])
                ln_apply = ctx.op(V, lambda v, par=par: v.tensor_scalar(
                    out=xnb[par][:], in0=xb[par][:], scalar1=st["rstd"][:],
                    scalar2=st["nmr"][:], op0=ALU.mult, op1=ALU.add), "dve")
                if i + 2 < 9:
                    ctx.wait(Y, "dve", ln_apply)
                    ctx.op(Y, lambda sy, i=i, par=par: sy.dma_start(
                        out=xb[par][:], in_=xin[t0s[i + 2]:t0s[i + 2] + 128, :]),
                        f"dx{par}", 16)
                    xb_ready[i + 2] = (f"dx{par}", ctx.c[f"dx{par}"])
                ctx.wait(T, "dc", consts)
                ctx.wait(T, "dve", ln_apply)
                for f in range(32):
                    tp = f % 2
                    ctx.wait(T, "dve", psT_free[tp])
                    pe_n = ctx.op(T, lambda t, par=par, tp=tp, f=f: t.transpose(
                        psT[tp][:], xnb[par][:, f * 128:(f + 1) * 128], id_sb[:]), "pe")
                    ctx.wait(V, "pe", pe_n)
                    psT_free[tp] = ctx.op(V, lambda v, i=i, tp=tp, f=f: v.tensor_copy(
                        xnt[:, f, t0s[i]:t0s[i] + 128], psT[tp][:]), "dve")
                xnb_free[par] = ctx.c["pe"]
            p1_pe = ctx.c["pe"]
            p1_dve = ctx.c["dve"]
            p1_act = ctx.c["act"]

        with ExitStack() as p2:
            c2 = sub(ctx, p2)
            csl = [c2.sb(f"csl{i}", [128, 32 * 512], BF16) for i in range(2)]
            psc = [c2.ps(f"psc{i}", [128, 512]) for i in range(8)]

            csl_ready = [None, None]
            csl_free = [0, 0]
            ctx.wait(G, "pe", p1_pe)
            ctx.wait(G, "dve", p1_dve)
            ctx.wait(G, "act", p1_act)
            ctx.wait(G, "dc", consts)
            ctx.op(G, lambda g: g.dma_start(out=csl[0][:], in_=convw[0, 0]), "dw0", 16)
            csl_ready[0] = ("dw0", ctx.c["dw0"])

            sl = 0
            for oc6 in range(6):
                for dt in range(3):
                    par = sl % 2
                    if sl + 1 < 18:
                        npar = (sl + 1) % 2
                        noc, ndt = divmod(sl + 1, 3)
                        ctx.wait(G, "pe", csl_free[npar])
                        ctx.op(G, lambda g, noc=noc, ndt=ndt, npar=npar: g.dma_start(
                            out=csl[npar][:], in_=convw[ndt, noc]), f"dw{npar}", 16)
                        csl_ready[npar] = (f"dw{npar}", ctx.c[f"dw{npar}"])
                    ctx.wait(T, *csl_ready[par])
                    if sl == 0:
                        ctx.wait(T, "dve", p1_dve)
                    for ab in range(NTB):
                        if dt == 0:
                            ctx.wait(T, "dve", qkvc_ready[ab])
                        for k in range(32):
                            is_last_of_slab = (ab == NTB - 1 and k == 31)
                            ctx.op(T, lambda t, ab=ab, k=k, dt=dt, par=par:
                                   t.matmul(psc[ab][:],
                                            xnt[:, k, ab0[ab] + dt - 1: ab0[ab] + dt + 127],
                                            csl[par][:, k * 512:(k + 1) * 512],
                                            start=(dt == 0 and k == 0), stop=False),
                                   "pe" if is_last_of_slab else None)
                    csl_free[par] = ctx.c["pe"]
                    sl += 1
                ctx.wait(T, "dve", ones_ready)
                for ab in range(NTB):
                    mm_n = ctx.op(T, lambda t, ab=ab, oc6=oc6: t.matmul(
                        psc[ab][:], ones_sb[:],
                        convb_sb[:, oc6 * 512:(oc6 + 1) * 512],
                        start=False, stop=True), "pe")
                    ctx.wait(V, "pe", mm_n)
                    qkvc_ready[ab] = ctx.op(V, lambda v, ab=ab, oc6=oc6: v.tensor_copy(
                        qkvc[:, ab, oc6 * 512:(oc6 + 1) * 512], psc[ab][:]), "dve")
            p2_pe = ctx.c["pe"]
            p2_dve = ctx.c["dve"]

    o2t_ready = [0] * NTB
    with ExitStack() as p3:
        c3 = sub(ctx, p3)
        o2t = c3.sb("o2t", [128, 8, TLOC], BF16)
        wq_sb = c3.sb("wq_sb", [128, 8, 1024], BF16)
        wk_sb = c3.sb("wk_sb", [128, 8, 1024], BF16)
        wv_sb = c3.sb("wv_sb", [128, 8, 1024], BF16)
        wo_sb = c3.sb("wo_sb", [128, 8, 1024], BF16)
        qkvT = [c3.sb(f"qkvT{i}", [128, 24, 128], BF16) for i in range(2)]
        qhT = [c3.sb(f"qhT{i}", [128, 8, 128], BF16) for i in range(2)]
        khT = [c3.sb(f"khT{i}", [128, 8, 128], BF16) for i in range(2)]
        vh = [c3.sb(f"vh{i}", [128, 16, 65], BF16) for i in range(2)]
        probs = c3.sb("probs", [128, 128], BF16)
        rz = c3.sb("rz", [128, 1], F32)
        oh = [c3.sb(f"oh{i}", [128, 16 * 64], BF16) for i in range(2)]
        o2 = [c3.sb(f"o2{i}", [128, 1024], BF16) for i in range(2)]
        psT3 = [c3.ps(f"psT3_{i}", [128, 128], BF16) for i in range(2)]
        psqk = [c3.ps(f"psqk{i}", [128, 128]) for i in range(2)]
        psv = c3.ps("psv", [128, 512])
        pss = c3.ps("pss", [128, 128])
        pso3 = c3.ps("pso3", [128, 65])
        pswo = c3.ps("pswo", [128, 512])

        for fn in (
            lambda g: g.dma_start(out=wq_sb[:], in_=wq.rearrange("(k p) o -> p k o", p=128)),
            lambda g: g.dma_start(out=wk_sb[:], in_=wk.rearrange("(k p) o -> p k o", p=128)),
            lambda g: g.dma_start(out=wv_sb[:], in_=wv.rearrange("(k p) o -> p k o", p=128)),
            lambda g: g.dma_start(out=wo_sb[:], in_=wo.rearrange("(k p) o -> p k o", p=128)),
        ):
            ctx.wait(G, "pe", p2_pe)
            ctx.wait(G, "dve", p2_dve)
            ctx.wait(G, "dc")
            ctx.op(G, fn, "dc", 16)
        w3_ready = ctx.c["dc"]
        for i in range(2):
            ctx.op(V, lambda v, i=i: v.memset(vh[i][:, :, 64:65], 1.0), "dve")
        vh_ones_dve = ctx.c["dve"]

        qkvT_free = [0, 0]
        psT3_free = [0, 0]
        qhT_free = [0, 0]
        khT_free = [0, 0]
        vh_free = [0, 0]
        oh_free = [0, 0]
        o2_free = [0, 0]
        psqk_free = [0, 0]
        psv_free = 0
        pss_free = 0
        pso3_free = 0
        pswo_free = 0
        probs_free = 0
        rz_free = 0

        ctx.wait(T, "dve", p2_dve)
        for ab in range(NTB):
            par = ab % 2
            ctx.wait(T, "dve", max(qkvc_ready[ab], qkvT_free[par]))
            for f in range(24):
                tp = f % 2
                ctx.wait(T, "dve", psT3_free[tp])
                pe_n = ctx.op(T, lambda t, ab=ab, f=f, tp=tp: t.transpose(
                    psT3[tp][:], qkvc[:, ab, f * 128:(f + 1) * 128], id_sb[:]), "pe")
                ctx.wait(V, "pe", pe_n)
                psT3_free[tp] = ctx.op(V, lambda v, par=par, f=f, tp=tp: v.tensor_copy(
                    qkvT[par][:, f, :], psT3[tp][:]), "dve")
            qkvT_ready = ctx.c["dve"]

            ctx.wait(T, "dc", w3_ready)
            for which, wsb, dsttile, dstfree, bias, sc in (
                    ("q", wq_sb, qhT, qhT_free, bq_sb, 0.125),
                    ("k", wk_sb, khT, khT_free, bk_sb, 1.0)):
                base = 0 if which == "q" else 8
                for hc in range(8):
                    pq = hc % 2
                    ctx.wait(T, "dve", qkvT_ready)
                    ctx.wait(T, "act", psqk_free[pq])
                    for k in range(8):
                        ctx.op(T, lambda t, k=k, hc=hc, par=par, pq=pq,
                               wsb=wsb, base=base: t.matmul(
                            psqk[pq][:], wsb[:, k, hc * 128:(hc + 1) * 128],
                            qkvT[par][:, base + k, :],
                            start=(k == 0), stop=(k == 7)), "pe" if k == 7 else None)
                    ctx.wait(S, "pe", ctx.c["pe"])
                    if which == "q":
                        ctx.wait(S, "dve", bq_ready)
                    if ab >= 2:
                        ctx.wait(S, "pe", dstfree[par])
                    psqk_free[pq] = ctx.op(S, lambda s, hc=hc, par=par, pq=pq,
                                           dsttile=dsttile, bias=bias, sc=sc:
                                           s.activation(
                        out=dsttile[par][:, hc, :], in_=psqk[pq][:], func=AF.Identity,
                        bias=bias[:, hc:hc + 1], scale=sc), "act")
            qk_ready = ctx.c["act"]

            for vc2 in range(2):
                ctx.wait(T, "dve", max(qkvT_ready, ones_ready))
                ctx.wait(T, "act", psv_free)
                for k in range(8):
                    ctx.op(T, lambda t, k=k, vc2=vc2, par=par: t.matmul(
                        psv[:], qkvT[par][:, 16 + k, :],
                        wv_sb[:, k, vc2 * 512:(vc2 + 1) * 512],
                        start=(k == 0), stop=False))
                mm = ctx.op(T, lambda t, vc2=vc2: t.matmul(
                    psv[:], ones_sb[:], bv_sb[:, vc2 * 512:(vc2 + 1) * 512],
                    start=False, stop=True), "pe")
                ctx.wait(S, "pe", mm)
                ctx.wait(S, "dve", vh_ones_dve)
                if ab >= 2:
                    ctx.wait(S, "pe", vh_free[par])
                psv_free = ctx.op(S, lambda s, vc2=vc2, par=par: s.activation(
                    out=vh[par][:, vc2 * 8:(vc2 + 1) * 8, 0:64],
                    in_=psv[:], func=AF.Copy), "act")
            v_ready = ctx.c["act"]

            for h in range(16):
                p0 = 64 * (h % 2)
                kt = h // 2
                ctx.wait(T, "act", max(qk_ready, pss_free))
                s_pe = ctx.op(T, lambda t, par=par, p0=p0, kt=kt: t.matmul(
                    pss[:], khT[par][p0:p0 + 64, kt, :], qhT[par][p0:p0 + 64, kt, :],
                    start=True, stop=True), "pe")
                ctx.wait(S, "pe", max(s_pe, probs_free))
                e_n = ctx.op(S, lambda s: s.activation(
                    out=probs[:], in_=pss[:], func=AF.Exp), "act")
                pss_free = e_n
                ctx.wait(V, "act", e_n)
                m_n = ctx.op(V, lambda v: v.tensor_tensor(
                    out=probs[:], in0=probs[:], in1=tri_sb[:], op=ALU.mult), "dve")
                ctx.wait(T, "dve", m_n)
                ctx.wait(T, "act", max(v_ready, pso3_free))
                pv_pe = ctx.op(T, lambda t, h=h, par=par: t.matmul(
                    pso3[:], probs[:], vh[par][:, h, :], start=True, stop=True), "pe")
                probs_free = pv_pe
                ctx.wait(V, "pe", pv_pe)
                ctx.wait(V, "act", rz_free)
                r_n = ctx.op(V, lambda v: v.reciprocal(rz[:], pso3[:, 64:65]), "dve")
                ctx.wait(S, "dve", r_n)
                if ab >= 2:
                    ctx.wait(S, "pe", oh_free[par])
                pso3_free = ctx.op(S, lambda s, h=h, par=par: s.activation(
                    out=oh[par][:, h * 64:(h + 1) * 64], in_=pso3[:, 0:64],
                    func=AF.Copy, scale=rz[:]), "act")
                rz_free = pso3_free
            oh_ready = ctx.c["act"]

            ctx.wait(T, "act", oh_ready)
            for f in range(8):
                tp = f % 2
                ctx.wait(T, "dve", psT3_free[tp])
                pe_n = ctx.op(T, lambda t, par=par, f=f, tp=tp: t.transpose(
                    psT3[tp][:], oh[par][:, f * 128:(f + 1) * 128], id_sb[:]), "pe")
                ctx.wait(V, "pe", pe_n)
                psT3_free[tp] = ctx.op(V, lambda v, par=par, f=f, tp=tp: v.tensor_copy(
                    qkvT[par][:, f, :], psT3[tp][:]), "dve")
            ohT_ready = ctx.c["dve"]
            oh_free[par] = ctx.c["pe"]

            for wc in range(2):
                ctx.wait(T, "dve", max(ohT_ready, ones_ready))
                ctx.wait(T, "act", pswo_free)
                for k in range(8):
                    ctx.op(T, lambda t, k=k, wc=wc, par=par: t.matmul(
                        pswo[:], qkvT[par][:, k, :],
                        wo_sb[:, k, wc * 512:(wc + 1) * 512],
                        start=(k == 0), stop=False))
                mm = ctx.op(T, lambda t, wc=wc: t.matmul(
                    pswo[:], ones_sb[:], bo_sb[:, wc * 512:(wc + 1) * 512],
                    start=False, stop=True), "pe")
                ctx.wait(S, "pe", mm)
                if ab >= 2:
                    ctx.wait(S, "pe", o2_free[par])
                pswo_free = ctx.op(S, lambda s, wc=wc, par=par: s.activation(
                    out=o2[par][:, wc * 512:(wc + 1) * 512], in_=pswo[:],
                    func=AF.Copy), "act")
            o2_ready = ctx.c["act"]

            ctx.wait(T, "act", o2_ready)
            for f in range(8):
                tp = f % 2
                ctx.wait(T, "dve", psT3_free[tp])
                pe_n = ctx.op(T, lambda t, par=par, f=f, tp=tp: t.transpose(
                    psT3[tp][:], o2[par][:, f * 128:(f + 1) * 128], id_sb[:]), "pe")
                ctx.wait(V, "pe", pe_n)
                psT3_free[tp] = ctx.op(V, lambda v, ab=ab, f=f, tp=tp: v.tensor_copy(
                    o2t[:, f, ab * 128:(ab + 1) * 128], psT3[tp][:]), "dve")
            o2t_ready[ab] = ctx.c["dve"]
            o2_free[par] = ctx.c["pe"]
            qkvT_free[par] = ctx.c["pe"]
            qhT_free[par] = ctx.c["pe"]
            khT_free[par] = ctx.c["pe"]
            vh_free[par] = ctx.c["pe"]
        p3_pe = ctx.c["pe"]
        p3_dve = ctx.c["dve"]
        p3_act = ctx.c["act"]

        with ExitStack() as p4:
            c4 = sub(ctx, p4)
            pjs = [c4.sb(f"pjs{i}", [128, 8 * 512], BF16) for i in range(2)]
            xsl = [c4.sb(f"p_xsl{i}", [128, 512], F32) for i in range(4)]
            xo_sb = [c4.sb(f"p_xo{i}", [128, 512], F32) for i in range(2)]
            psp = [psv, pswo]

            pjs_ready = [None, None]
            pjs_free = [0, 0]
            ctx.wait(T, "act", p3_act)
            ctx.wait(T, "dve", p3_dve)
            ctx.wait(G, "pe", p3_pe)
            ctx.wait(G, "dve", p3_dve)
            ctx.wait(G, "act", p3_act)
            ctx.wait(G, "dw0")
            ctx.wait(G, "dw1")
            ctx.op(G, lambda g: g.dma_start(out=pjs[0][:], in_=projw[0]), "dw0", 16)
            pjs_ready[0] = ("dw0", ctx.c["dw0"])

            psp_free = [0, 0]
            xsl_free = [0] * 4
            xo_free = [0, 0]
            it = 0
            for oc in range(8):
                wp = oc % 2
                if oc + 1 < 8:
                    npar = (oc + 1) % 2
                    ctx.wait(G, "pe", pjs_free[npar])
                    ctx.op(G, lambda g, oc=oc, npar=npar: g.dma_start(
                        out=pjs[npar][:], in_=projw[oc + 1]), f"dw{npar}", 16)
                    pjs_ready[npar] = (f"dw{npar}", ctx.c[f"dw{npar}"])
                for tb in range(NTB):
                    par = it % 2
                    xp = it % 4
                    ctx.wait(Y, "dve", xsl_free[xp])
                    r0 = xin_row(tb * 128)
                    ctx.op(Y, lambda sy, oc=oc, xp=xp, r0=r0: sy.dma_start(
                        out=xsl[xp][:], in_=xin[r0:r0 + 128, oc * 512:(oc + 1) * 512]),
                        f"dxs{xp}", 16)
                    xsl_n = (f"dxs{xp}", ctx.c[f"dxs{xp}"])

                    ctx.wait(T, *pjs_ready[wp])
                    ctx.wait(T, "dve", max(o2t_ready[tb], psp_free[par], ones_ready))
                    for k in range(8):
                        ctx.op(T, lambda t, tb=tb, k=k, par=par, wp=wp: t.matmul(
                            psp[par][:], o2t[:, k, tb * 128:(tb + 1) * 128],
                            pjs[wp][:, k * 512:(k + 1) * 512],
                            start=(k == 0), stop=False))
                    mm_n = ctx.op(T, lambda t, oc=oc, par=par: t.matmul(
                        psp[par][:], ones_sb[:], projb_sb[:, oc * 512:(oc + 1) * 512],
                        start=False, stop=True), "pe")
                    if tb == NTB - 1:
                        pjs_free[wp] = mm_n

                    ctx.wait(V, "pe", mm_n)
                    ctx.wait(V, *xsl_n)
                    ctx.wait(V, f"dst{par}", xo_free[par])
                    ep_n = ctx.op(V, lambda v, par=par, xp=xp: v.tensor_tensor(
                        out=xo_sb[par][:], in0=psp[par][:], in1=xsl[xp][:],
                        op=ALU.add), "dve")
                    psp_free[par] = ep_n
                    xsl_free[xp] = ep_n
                    ctx.wait(Y, "dve", ep_n)
                    xo_free[par] = ctx.op(Y, lambda sy, tb=tb, oc=oc, par=par: sy.dma_start(
                        out=xout[tb * 128:(tb + 1) * 128, oc * 512:(oc + 1) * 512],
                        in_=xo_sb[par][:]), f"dst{par}", 16)
                    it += 1


def stage_l0attn_weights(inp):
    cw = np.asarray(inp["L0_attn_conv_w"], np.float32)
    convw = np.stack([pack_slabs(np.ascontiguousarray(cw[dt]), 512) for dt in range(3)])
    wq = np.ascontiguousarray(np.asarray(inp["L0_attn_wq"], np.float32).reshape(1024, 1024))
    wk = np.ascontiguousarray(np.asarray(inp["L0_attn_wk"], np.float32).reshape(1024, 1024))
    wv = np.ascontiguousarray(np.asarray(inp["L0_attn_wv"], np.float32).reshape(1024, 1024))
    wo = np.ascontiguousarray(np.asarray(inp["L0_attn_wo"], np.float32).reshape(1024, 1024))
    bq = np.asarray(inp["L0_attn_bq"], np.float32).reshape(1024)
    bk = np.asarray(inp["L0_attn_bk"], np.float32).reshape(1024)
    bv = np.asarray(inp["L0_attn_bv"], np.float32).reshape(1024)
    bo = np.asarray(inp["L0_attn_bo"], np.float32)
    pw = np.asarray(inp["L0_attn_proj_w"], np.float32)
    pb = np.asarray(inp["L0_attn_proj_b"], np.float32)
    cb = np.asarray(inp["L0_attn_conv_b"], np.float32)
    k_idx, q_idx = np.meshgrid(np.arange(128), np.arange(128), indexing="ij")
    tri_kq = (k_idx <= q_idx).astype(np.float32)
    return {
        "convw_s": convw,
        "convb": cb.reshape(1, 3072),
        "wq_r": wq, "wk_r": wk, "wv_r": wv, "wo_r": wo,
        "bq_c": np.ascontiguousarray(bq.reshape(8, 128).T),
        "bk_c": np.ascontiguousarray(bk.reshape(8, 128).T),
        "bv_r": bv.reshape(1, 1024),
        "bo_r": bo.reshape(1, 1024),
        "projw_s": pack_slabs(pw, 512),
        "projb": pb.reshape(1, D),
        "ident": np.eye(128, dtype=np.float32),
        "trimask": tri_kq,
    }


def slice_haloed(x_full, core):
    b, (a0, a1), (b0, b1) = chunk_ranges(core)
    out = np.zeros((TIN, D), x_full.dtype)
    out[1:513] = x_full[b, a0:a1]
    out[515:1027] = x_full[b, b0:b1]
    if a0 > 0:
        out[0] = x_full[b, a0 - 1]
    if a1 < 4096:
        out[513] = x_full[b, a1]
    if b0 > 0:
        out[514] = x_full[b, b0 - 1]
    if b1 < 4096:
        out[1027] = x_full[b, b1]
    return out


def build_l1ffn_program():
    nc = bass.Bass("TRN2", target_bir_lowering=False)
    xin = nc.dram_tensor("xin", [TLOC, D], F32, kind="ExternalInput").ap()
    a2 = nc.dram_tensor("a2", [128, 64], F32, kind="ExternalInput").ap()
    b2 = nc.dram_tensor("b2", [128, 64], F32, kind="ExternalInput").ap()
    betar = nc.dram_tensor("betar", [64, 64], F32, kind="ExternalInput").ap()
    gcols = nc.dram_tensor("gcols", [128, 32], F32, kind="ExternalInput").ap()
    biasr = nc.dram_tensor("biasr", [1, D], F32, kind="ExternalInput").ap()
    ident = nc.dram_tensor("ident", [128, 128], F32, kind="ExternalInput").ap()
    xout = nc.dram_tensor("xout", [TLOC, D], F32, kind="ExternalOutput").ap()
    cdram = nc.dram_tensor("c_scratch", [64, 64], F32).ap()

    with ExitStack() as es:
        ctx = Ctx(nc, es)
        emit_l1ffn(ctx, xin, xout, a2, b2, betar, gcols, biasr, ident, cdram)
        ctx.wait("sync", "dst0")
        ctx.wait("sync", "dst1")
        ctx.flush()
    return nc


def emit_l1ffn(ctx, xin, xout, a2, b2, betar, gcols, biasr, ident, cdram):
    ctx.sems("dc", "dcy", "dcy2", "dx0", "dx1", "dw0", "pe", "act", "dve", "dst0", "dst1")
    T, V, S, G, Y = "tensor", "vector", "scalar", "gpsimd", "sync"

    id_sb = ctx.sb("f_id", [128, 128], BF16)
    a2_sb = ctx.sb("f_a2", [128, 64], BF16)
    b2_sb = ctx.sb("f_b2", [128, 64], BF16)
    betar_sb = ctx.sb("f_betar", [64, 64], BF16)
    gc_sb = ctx.sb("f_gc", [128, 32], F32)
    bias_bc = ctx.sb("f_biasbc", [128, D], F32)
    cb2 = ctx.sb("f_cb2", [128, D], BF16)
    m1_sb = ctx.sb("f_m1", [64, 64], BF16)
    m1t_sb = ctx.sb("f_m1t", [64, 64], BF16)
    c_sb = ctx.sb("f_c", [64, 64], F32)

    xb = [ctx.sb(f"f_xb{i}", [128, D], F32) for i in range(2)]
    xn2 = [ctx.sb(f"f_xn2{i}", [128, D], BF16) for i in range(2)]
    hT = [ctx.sb(f"f_hT{i}", [128, 32, 128], BF16) for i in range(2)]
    u_sb = [ctx.sb(f"f_u{i}", [128, D], BF16) for i in range(2)]
    uT = [ctx.sb(f"f_uT{i}", [128, 32, 128], BF16) for i in range(2)]
    y_sb = [ctx.sb(f"f_y{i}", [128, D], F32) for i in range(2)]
    g_sb = ctx.sb("f_g", [128, D], F32)
    e1_sb = ctx.sb("f_e1", [128, D], F32)
    xo_sb = y_sb
    st = alloc_stats(ctx, "f_st", sq=xn2[0])
    r1c = ctx.sb("f_r1", [128, 1], F32)
    v2e = ctx.sb("f_v2e", [128, 1], F32)
    r2c = ctx.sb("f_r2", [128, 1], F32)
    rstd2 = ctx.sb("f_rstd2", [128, 1], F32)
    s2col = ctx.sb("f_s2col", [128, 1], F32)
    Rc = ctx.sb("f_R", [128, 1], F32)
    nmR = ctx.sb("f_nmR", [128, 1], F32)

    psT = [ctx.ps(f"f_psT{i}", [128, 128], BF16) for i in range(2)]
    psu = [ctx.ps(f"f_psu{i}", [128, 64]) for i in range(2)]
    psy = [ctx.ps(f"f_psy{i}", [128, 64]) for i in range(2)]
    psc1 = ctx.ps("f_psc1", [64, 64])
    psc2 = ctx.ps("f_psc2", [64, 64], BF16)

    for fn in (
        lambda g: g.dma_start(out=id_sb[:], in_=ident[:]),
        lambda g: g.dma_start(out=a2_sb[:], in_=a2[:]),
        lambda g: g.dma_start(out=b2_sb[:], in_=b2[:]),
        lambda g: g.dma_start(out=betar_sb[:], in_=betar[:]),
        lambda g: g.dma_start(out=gc_sb[:], in_=gcols[:]),
    ):
        ctx.wait(G, "dc")
        ctx.op(G, fn, "dc", 16)
    consts = ctx.c["dc"]
    ctx.op(Y, lambda sy: sy.dma_start(out=bias_bc[:], in_=biasr.broadcast_to([128, D])), "dcy", 16)
    bias_ready = ("dcy", ctx.c["dcy"])

    ctx.wait(T, "dc", consts)
    n = ctx.op(T, lambda t: t.matmul(psc1[:], b2_sb[0:64, :], betar_sb[:],
                                     start=True, stop=True), "pe")
    ctx.wait(V, "pe", n)
    n = ctx.op(V, lambda v: v.tensor_copy(m1_sb[:], psc1[:]), "dve")
    ctx.wait(T, "dve", n)
    n = ctx.op(T, lambda t: t.transpose(psc2[:], m1_sb[:], id_sb[0:64, 0:64]), "pe")
    ctx.wait(V, "pe", n)
    n = ctx.op(V, lambda v: v.tensor_copy(m1t_sb[:], psc2[:]), "dve")
    ctx.wait(T, "dve", n)
    n = ctx.op(T, lambda t: t.matmul(psc1[:], m1t_sb[:], a2_sb[0:64, :],
                                     start=True, stop=True), "pe")
    ctx.wait(V, "pe", n)
    n = ctx.op(V, lambda v: v.tensor_copy(c_sb[:], psc1[:]), "dve")
    ctx.wait(Y, "dve", n)
    ctx.op(Y, lambda sy: sy.dma_start(out=cdram[:], in_=c_sb[:]), "dcy2", 16)
    ctx.wait(G, "dcy2")
    ctx.wait(G, "dc", consts)
    ctx.op(G, lambda g: g.dma_start(
        out=cb2[:], in_=cdram.rearrange("q b -> (q b)").unsqueeze(0).broadcast_to([128, D])), "dc", 16)
    cb2_ready = ("dc", ctx.c["dc"])

    xb_ready = [None] * NTB
    for i in range(2):
        ctx.op(Y, lambda sy, i=i: sy.dma_start(
            out=xb[i][:], in_=xin[i * 128:(i + 1) * 128, :]), f"dx{i % 2}", 16)
        xb_ready[i] = (f"dx{i % 2}", ctx.c[f"dx{i % 2}"])

    xn2_free = [0, 0]
    hT_free = [0, 0]
    uT_free = [0, 0]
    u_free = [0, 0]
    psT_free = [0, 0]
    psT_free_sem = [None, None]
    psu_free = [0, 0]
    psy_free = [0, 0]
    y_free = [0, 0]
    xo_free = [0, 0]
    xb_free = [0, 0]
    xn2_freed_dve = [0, 0]

    for tb in range(NTB):
        par = tb % 2
        if tb >= 2:
            ctx.wait(S, "pe", xn2_free[par])
            ctx.wait(S, "dve", xn2_freed_dve[par])
        emit_ln_stats(ctx, xb[par], st, sq=xn2[par], ready=xb_ready[tb])
        ctx.wait(V, "dve")
        ctx.op(V, lambda v: v.tensor_scalar(out=v2e[:], in0=st["rvar"][:],
                                            scalar1=-LN_EPS, scalar2=1.0 + LN_EPS,
                                            op0=ALU.mult, op1=ALU.add), "dve")
        ctx.wait(V, "dve")
        ctx.op(V, lambda v: v.reciprocal(r2c[:], v2e[:]), "dve")
        ctx.wait(S, "dve")
        ctx.op(S, lambda s: s.activation(out=rstd2[:], in_=r2c[:], func=AF.Sqrt), "act")
        ctx.op(S, lambda s: s.activation(out=s2col[:], in_=v2e[:], func=AF.Sqrt), "act")
        ctx.wait(V, "act")
        ctx.op(V, lambda v: v.tensor_tensor(out=Rc[:], in0=st["rstd"][:],
                                            in1=rstd2[:], op=ALU.mult), "dve")
        ctx.wait(V, "dve")
        ctx.op(V, lambda v: v.scalar_tensor_tensor(
            out=nmR[:], in0=st["mu"][:], scalar=-1.0, in1=Rc[:],
            op0=ALU.mult, op1=ALU.mult), "dve")
        ctx.wait(V, "dve")
        if tb >= 2:
            ctx.wait(V, "pe", xn2_free[par])
        xn2_n = ctx.op(V, lambda v, par=par: v.tensor_scalar(
            out=xn2[par][:], in0=xb[par][:], scalar1=Rc[:], scalar2=nmR[:],
            op0=ALU.mult, op1=ALU.add), "dve")

        ctx.wait(T, "dc", consts)
        ctx.wait(T, "dve", xn2_n)
        for f in range(32):
            tp = f % 2
            if psT_free_sem[tp]:
                ctx.wait(T, psT_free_sem[tp], psT_free[tp])
            if tb >= 2 and f < 2:
                ctx.wait(T, "pe", hT_free[par])
            pe_n = ctx.op(T, lambda t, par=par, tp=tp, f=f: t.transpose(
                psT[tp][:], xn2[par][:, f * 128:(f + 1) * 128], id_sb[:]), "pe")
            ctx.wait(S, "pe", pe_n)
            if tb >= 2:
                ctx.wait(S, "pe", hT_free[par])
            psT_free[tp] = ctx.op(S, lambda s, par=par, tp=tp, f=f: s.activation(
                out=hT[par][:, f, :], in_=psT[tp][:], func=AF.Identity,
                scale=gc_sb[:, f:f + 1]), "act")
            psT_free_sem[tp] = "act"
        xn2_free[par] = ctx.c["pe"]
        hT_ready = ctx.c["act"]

        ctx.wait(T, "act", hT_ready)
        for p in range(64):
            pp = p % 2
            base = 64 * (p % 2)
            f = p // 2
            ctx.wait(T, "dve", psu_free[pp])
            mm = ctx.op(T, lambda t, par=par, f=f, base=base, pp=pp: t.matmul(
                psu[pp][:], hT[par][base:base + 64, f, :], a2_sb[base:base + 64, :],
                start=True, stop=True), "pe")
            ctx.wait(V, "pe", mm)
            psu_free[pp] = ctx.op(V, lambda v, par=par, p=p, pp=pp: v.tensor_copy(
                u_sb[par][:, p::64], psu[pp][:]), "dve")
        u_ready = ctx.c["dve"]

        ctx.wait(T, "dve", u_ready)
        for f in range(32):
            tp = f % 2
            ctx.wait(T, psT_free_sem[tp], psT_free[tp])
            if tb >= 2 and f < 2:
                ctx.wait(T, "pe", uT_free[par])
            pe_n = ctx.op(T, lambda t, par=par, tp=tp, f=f: t.transpose(
                psT[tp][:], u_sb[par][:, f * 128:(f + 1) * 128], id_sb[:]), "pe")
            ctx.wait(V, "pe", pe_n)
            if tb >= 2:
                ctx.wait(V, "pe", uT_free[par])
            psT_free[tp] = ctx.op(V, lambda v, par=par, tp=tp, f=f: v.tensor_copy(
                uT[par][:, f, :], psT[tp][:]), "dve")
            psT_free_sem[tp] = "dve"
        uT_ready = ctx.c["dve"]
        u_freed_dve_cur = ctx.c["pe"]

        ctx.wait(T, "dve", uT_ready)
        ctx.wait(V, *cb2_ready)
        for b in range(64):
            pp = b % 2
            base = 64 * (b % 2)
            f = b // 2
            ctx.wait(T, "dve", psy_free[pp])
            mm = ctx.op(T, lambda t, par=par, f=f, base=base, pp=pp: t.matmul(
                psy[pp][:], uT[par][base:base + 64, f, :], b2_sb[base:base + 64, :],
                start=True, stop=True), "pe")
            ctx.wait(V, "pe", mm)
            if b < 2 and tb >= 2:
                ctx.wait(V, "act", y_free[par])
                ctx.wait(V, f"dst{par}", xo_free[par])
            psy_free[pp] = ctx.op(V, lambda v, par=par, b=b, pp=pp: v.scalar_tensor_tensor(
                out=y_sb[par][:, b::64], in0=psy[pp][:], scalar=1.0,
                in1=cb2[:, b::64], op0=ALU.mult, op1=ALU.add), "dve")
        y_ready = ctx.c["dve"]

        ctx.wait(S, "dve", y_ready)
        gelu_n = ctx.op(S, lambda s, par=par: s.activation(
            out=g_sb[:], in_=y_sb[par][:], func=AF.Gelu), "act")
        y_free[par] = gelu_n
        ctx.wait(V, "act", gelu_n)
        ctx.wait(V, *bias_ready)
        e1_n = ctx.op(V, lambda v: v.tensor_tensor(
            out=e1_sb[:], in0=g_sb[:], in1=bias_bc[:], op=ALU.add), "dve")
        ctx.wait(V, "dve", e1_n)
        e2_n = ctx.op(V, lambda v, par=par: v.scalar_tensor_tensor(
            out=e1_sb[:], in0=xn2[par][:], scalar=s2col[:], in1=e1_sb[:],
            op0=ALU.mult, op1=ALU.add), "dve")
        ctx.wait(V, "dve", e2_n)
        e3_n = ctx.op(V, lambda v, par=par: v.scalar_tensor_tensor(
            out=xo_sb[par][:], in0=e1_sb[:], scalar=0.5, in1=xb[par][:],
            op0=ALU.mult, op1=ALU.add), "dve")
        xb_free[par] = e3_n
        ctx.wait(Y, "dve", e3_n)
        xo_free[par] = ctx.op(Y, lambda sy, tb=tb, par=par: sy.dma_start(
            out=xout[tb * 128:(tb + 1) * 128, :], in_=xo_sb[par][:]), f"dst{par}", 16)

        if tb + 2 < NTB:
            ctx.wait(Y, "dve", e3_n)
            ctx.op(Y, lambda sy, tb=tb, par=par: sy.dma_start(
                out=xb[par][:], in_=xin[(tb + 2) * 128:(tb + 3) * 128, :]), f"dx{par}", 16)
            xb_ready[tb + 2] = (f"dx{par}", ctx.c[f"dx{par}"])

        hT_free[par] = ctx.c["pe"]
        uT_free[par] = ctx.c["pe"]
        xn2_freed_dve[par] = e2_n


def stage_l1ffn_weights(inp):
    A = np.asarray(inp["L1_ffn_A"], np.float32)
    B = np.asarray(inp["L1_ffn_B"], np.float32)
    beta = np.asarray(inp["L1_ffn_ln_beta"], np.float32)
    gamma = np.asarray(inp["L1_ffn_ln_gamma"], np.float32)
    bias = np.asarray(inp["L1_ffn_bias"], np.float32)
    return {
        "a2": np.ascontiguousarray(np.concatenate([A, A], axis=0)),
        "b2": np.ascontiguousarray(np.concatenate([B, B], axis=0)),
        "betar": np.ascontiguousarray(beta.reshape(64, 64)),
        "gcols": np.ascontiguousarray(gamma.reshape(32, 128).T),
        "biasr": bias.reshape(1, D),
        "ident": np.eye(128, dtype=np.float32),
    }


def build_l1attn_a_program():
    nc = bass.Bass("TRN2", target_bir_lowering=False)
    xin = nc.dram_tensor("xin", [TLOC, D], F32, kind="ExternalInput").ap()
    wqs = nc.dram_tensor("wq_s", [8, 128, 32 * 512], F32, kind="ExternalInput").ap()
    wks = nc.dram_tensor("wk_s", [8, 128, 32 * 512], F32, kind="ExternalInput").ap()
    wvs = nc.dram_tensor("wv_s", [8, 128, 32 * 512], F32, kind="ExternalInput").ap()
    cs = nc.dram_tensor("cs_cols", [TLOC, 6], F32, kind="ExternalInput").ap()
    ident = nc.dram_tensor("ident", [128, 128], F32, kind="ExternalInput").ap()
    qt = nc.dram_tensor("qt", [128, 32, TLOC], BF16, kind="ExternalOutput").ap()
    kt = nc.dram_tensor("kt", [128, 32, TLOC], BF16, kind="ExternalOutput").ap()
    vt = nc.dram_tensor("vt", [TLOC, D], BF16, kind="ExternalOutput").ap()

    with ExitStack() as es:
        ctx = Ctx(nc, es)
        emit_l1attn_a(ctx, xin, wqs, wks, wvs, cs, ident, qt, kt, vt)
        ctx.wait("sync", "dst0")
        ctx.wait("sync", "dst1")
        ctx.flush()
    return nc


def emit_l1attn_a(ctx, xin, wqs, wks, wvs, cs, ident, qt, kt, vt):
    ctx.sems("dc", "dx0", "dx1", "dw0", "dw1", "pe", "act", "dve", "dst0", "dst1")
    T, V, S, G, Y = "tensor", "vector", "scalar", "gpsimd", "sync"

    id_sb = ctx.sb("q_id", [128, 128], BF16)
    cs_sb = ctx.sb("q_cs", [128, NTB, 6], F32)
    xnt = ctx.sb("q_xnt", [128, 32, TLOC], BF16)
    wsl = [ctx.sb(f"q_wsl{i}", [128, 32 * 512], BF16) for i in range(2)]
    ro_sb = [ctx.sb(f"q_ro{i}", [128, 512], BF16) for i in range(2)]
    tt_sb = [ctx.sb(f"q_tt{i}", [128, 128], BF16) for i in range(2)]

    psT = [ctx.ps(f"q_psT{i}", [128, 128], BF16) for i in range(2)]
    pso = [ctx.ps(f"q_pso{i}", [128, 512]) for i in range(2)]

    ctx.wait(G, "dc")
    ctx.op(G, lambda g: g.dma_start(out=id_sb[:], in_=ident[:]), "dc", 16)
    ctx.wait(G, "dc")
    ctx.op(G, lambda g: g.dma_start(
        out=cs_sb[:], in_=cs.rearrange("(tb p) c -> p tb c", p=128)), "dc", 16)
    consts = ctx.c["dc"]

    with ExitStack() as p1:
        c1 = sub(ctx, p1)
        xb = [c1.sb(f"q_xb{i}", [128, D], F32) for i in range(2)]
        xnb = [c1.sb(f"q_xnb{i}", [128, D], BF16) for i in range(2)]
        st = alloc_stats(c1, "q_st", sq=xnb[0])

        xb_ready = [None] * NTB
        for i in range(2):
            ctx.op(Y, lambda sy, i=i: sy.dma_start(
                out=xb[i][:], in_=xin[i * 128:(i + 1) * 128, :]), f"dx{i % 2}", 16)
            xb_ready[i] = (f"dx{i % 2}", ctx.c[f"dx{i % 2}"])
        xnb_free = [0, 0]
        psT_free = [0, 0]
        for i in range(NTB):
            par = i % 2
            if i >= 2:
                ctx.wait(S, "pe", xnb_free[par])
            emit_ln_stats(ctx, xb[par], st, sq=xnb[par], ready=xb_ready[i])
            ctx.wait(V, "dve")
            if i >= 2:
                ctx.wait(V, "pe", xnb_free[par])
            ln_apply = ctx.op(V, lambda v, par=par: v.tensor_scalar(
                out=xnb[par][:], in0=xb[par][:], scalar1=st["rstd"][:],
                scalar2=st["nmr"][:], op0=ALU.mult, op1=ALU.add), "dve")
            if i + 2 < NTB:
                ctx.wait(Y, "dve", ln_apply)
                ctx.op(Y, lambda sy, i=i, par=par: sy.dma_start(
                    out=xb[par][:], in_=xin[(i + 2) * 128:(i + 3) * 128, :]), f"dx{par}", 16)
                xb_ready[i + 2] = (f"dx{par}", ctx.c[f"dx{par}"])
            ctx.wait(T, "dc", consts)
            ctx.wait(T, "dve", ln_apply)
            for f in range(32):
                tp = f % 2
                ctx.wait(T, "dve", psT_free[tp])
                pe_n = ctx.op(T, lambda t, par=par, tp=tp, f=f: t.transpose(
                    psT[tp][:], xnb[par][:, f * 128:(f + 1) * 128], id_sb[:]), "pe")
                ctx.wait(V, "pe", pe_n)
                psT_free[tp] = ctx.op(V, lambda v, i=i, tp=tp, f=f: v.tensor_copy(
                    xnt[:, f, i * 128:(i + 1) * 128], psT[tp][:]), "dve")
            xnb_free[par] = ctx.c["pe"]
        p1_pe = ctx.c["pe"]
        p1_dve = ctx.c["dve"]
        p1_act = ctx.c["act"]

    wsl_ready = [None, None]
    wsl_free = [0, 0]
    pso_free = [0, 0]
    ro_free = [0, 0]
    ro_free_sem = [None, None]
    tt_free = [0, 0]
    psT2_free = [0, 0]

    ctx.wait(G, "pe", p1_pe)
    ctx.wait(G, "dve", p1_dve)
    ctx.wait(G, "act", p1_act)

    sl = 0
    for wi, (wsrc, dst) in enumerate(((wqs, qt), (wks, kt), (wvs, vt))):
        is_v = wi == 2
        csel = (0, 3, None)[wi]
        for oc in range(8):
            par = sl % 2
            if sl == 0:
                ctx.wait(G, "dc", consts)
                ctx.op(G, lambda g: g.dma_start(out=wsl[0][:], in_=wqs[0]), "dw0", 16)
                wsl_ready[0] = ("dw0", ctx.c["dw0"])
            if sl + 1 < 24:
                npar = (sl + 1) % 2
                nwi, noc = divmod(sl + 1, 8)
                nsrc = (wqs, wks, wvs)[nwi]
                ctx.wait(G, "pe", wsl_free[npar])
                ctx.op(G, lambda g, nsrc=nsrc, noc=noc, npar=npar: g.dma_start(
                    out=wsl[npar][:], in_=nsrc[noc]), f"dw{npar}", 16)
                wsl_ready[npar] = (f"dw{npar}", ctx.c[f"dw{npar}"])

            for tb in range(NTB):
                pp = (sl * NTB + tb) % 2
                ctx.wait(T, *wsl_ready[par])
                ctx.wait(T, "dve", pso_free[pp])
                for k in range(32):
                    mm = ctx.op(T, lambda t, tb=tb, k=k, par=par, pp=pp: t.matmul(
                        pso[pp][:], xnt[:, k, tb * 128:(tb + 1) * 128],
                        wsl[par][:, k * 512:(k + 1) * 512],
                        start=(k == 0), stop=(k == 31)), "pe" if k == 31 else None)
                if tb == NTB - 1:
                    wsl_free[par] = mm

                if is_v:
                    ctx.wait(V, "pe", mm)
                    rpar = pp
                    if ro_free_sem[rpar]:
                        ctx.wait(V, ro_free_sem[rpar], ro_free[rpar])
                    ev = ctx.op(V, lambda v, rpar=rpar, pp=pp: v.tensor_copy(
                        ro_sb[rpar][:], pso[pp][:]), "dve")
                    pso_free[pp] = ev
                    ctx.wait(Y, "dve", ev)
                    ro_free[rpar] = ctx.op(Y, lambda sy, tb=tb, oc=oc, rpar=rpar: sy.dma_start(
                        out=vt[tb * 128:(tb + 1) * 128, oc * 512:(oc + 1) * 512],
                        in_=ro_sb[rpar][:]), f"dst{rpar}", 16)
                    ro_free_sem[rpar] = f"dst{rpar}"
                else:
                    rpar = pp
                    ctx.wait(V, "pe", mm)
                    if ro_free_sem[rpar]:
                        ctx.wait(V, ro_free_sem[rpar], ro_free[rpar])
                    cc = cs_sb[:, tb, csel:csel + 1]
                    ss_ = cs_sb[:, tb, csel + 1:csel + 2]
                    ns_ = cs_sb[:, tb, csel + 2:csel + 3]
                    ctx.wait(V, "dc", consts)
                    ctx.op(V, lambda v, rpar=rpar, pp=pp, cc=cc: v.tensor_scalar(
                        out=ro_sb[rpar][:, 0:512:2], in0=pso[pp][:, 0:512:2],
                        scalar1=cc, scalar2=None, op0=ALU.mult), "dve")
                    ctx.op(V, lambda v, rpar=rpar, pp=pp, cc=cc: v.tensor_scalar(
                        out=ro_sb[rpar][:, 1:512:2], in0=pso[pp][:, 1:512:2],
                        scalar1=cc, scalar2=None, op0=ALU.mult), "dve")
                    ctx.wait(V, "dve")
                    ctx.op(V, lambda v, rpar=rpar, pp=pp, ss_=ss_: v.scalar_tensor_tensor(
                        out=ro_sb[rpar][:, 1:512:2], in0=pso[pp][:, 0:512:2],
                        scalar=ss_, in1=ro_sb[rpar][:, 1:512:2],
                        op0=ALU.mult, op1=ALU.add), "dve")
                    ctx.wait(V, "dve")
                    ev = ctx.op(V, lambda v, rpar=rpar, pp=pp, ns_=ns_: v.scalar_tensor_tensor(
                        out=ro_sb[rpar][:, 0:512:2], in0=pso[pp][:, 1:512:2],
                        scalar=ns_, in1=ro_sb[rpar][:, 0:512:2],
                        op0=ALU.mult, op1=ALU.add), "dve")
                    pso_free[pp] = ev
                    for hh in range(4):
                        tp = hh % 2
                        ctx.wait(T, "dve", max(ev, psT2_free[tp]))
                        pe_n = ctx.op(T, lambda t, rpar=rpar, hh=hh, tp=tp: t.transpose(
                            psT[tp][:], ro_sb[rpar][:, hh * 128:(hh + 1) * 128],
                            id_sb[:]), "pe")
                        ctx.wait(V, "pe", pe_n)
                        tpar = hh % 2
                        if tt_free[tpar]:
                            ctx.wait(V, f"dst{tpar}", tt_free[tpar])
                        psT2_free[tp] = ctx.op(V, lambda v, tp=tp, tpar=tpar: v.tensor_copy(
                            tt_sb[tpar][:], psT[tp][:]), "dve")
                        ctx.wait(Y, "dve", psT2_free[tp])
                        tt_free[tpar] = ctx.op(Y, lambda sy, dst=dst, oc=oc, hh=hh,
                                               tb=tb, tpar=tpar: sy.dma_start(
                            out=dst[0:128, oc * 4 + hh, tb * 128:(tb + 1) * 128],
                            in_=tt_sb[tpar][:]), f"dst{tpar}", 16)
                    ro_free[rpar] = ctx.c["pe"]
                    ro_free_sem[rpar] = "pe"
            sl += 1


def stage_l1attn_a_weights(inp):
    wq = np.asarray(inp["L1_attn_wq"], np.float32)
    wk = np.asarray(inp["L1_attn_wk"], np.float32)
    wv = np.asarray(inp["L1_attn_wv"], np.float32)
    return {
        "wq_s": pack_slabs(np.ascontiguousarray(wq.T), 512),
        "wk_s": pack_slabs(np.ascontiguousarray(wk.T), 512),
        "wv_s": pack_slabs(np.ascontiguousarray(wv.T), 512),
        "ident": np.eye(128, dtype=np.float32),
    }


def cs_cols_for(core):
    b, (a0, a1), (b0, b1) = chunk_ranges(core)
    pos = np.concatenate([np.arange(a0, a1), np.arange(b0, b1)]).astype(np.float32)
    c = np.cos(pos)
    s = np.sin(pos)
    sc = 1.0 / np.sqrt(128.0)
    return np.stack([c * sc, s * sc, -s * sc, c, s, -s], axis=1).astype(np.float32)


NKA, NKB = 12, 28


def build_l1attn_b_program():
    nc = bass.Bass("TRN2", target_bir_lowering=False)
    xin = nc.dram_tensor("xin", [TLOC, D], F32, kind="ExternalInput").ap()
    qt = nc.dram_tensor("qt", [128, 32, TLOC], BF16, kind="ExternalInput").ap()
    kta = nc.dram_tensor("kta", [128, 32, NKA * 128], BF16, kind="ExternalInput").ap()
    ktb = nc.dram_tensor("ktb", [128, 32, NKB * 128], BF16, kind="ExternalInput").ap()
    ktd = nc.dram_tensor("ktd", [128, 32, TLOC], BF16, kind="ExternalInput").ap()
    vta = nc.dram_tensor("vta", [32, 128, NKA * 129], BF16, kind="ExternalInput").ap()
    vtb = nc.dram_tensor("vtb", [32, 128, NKB * 129], BF16, kind="ExternalInput").ap()
    vtd = nc.dram_tensor("vtd", [32, 128, 8 * 129], BF16, kind="ExternalInput").ap()
    wos = nc.dram_tensor("wo_s", [8, 128, 32 * 512], F32, kind="ExternalInput").ap()
    ident = nc.dram_tensor("ident", [128, 128], F32, kind="ExternalInput").ap()
    trimask = nc.dram_tensor("trimask", [128, 128], F32, kind="ExternalInput").ap()
    xout = nc.dram_tensor("xout", [TLOC, D], F32, kind="ExternalOutput").ap()

    with ExitStack() as es:
        ctx = Ctx(nc, es)
        emit_l1attn_b(ctx, xin, qt, kta, ktb, ktd, vta, vtb, vtd, wos,
                      ident, trimask, xout)
        ctx.wait("sync", "dst0")
        ctx.wait("sync", "dst1")
        ctx.flush()
    return nc


def emit_l1attn_b(ctx, xin, qt, kta, ktb, ktd, vta, vtb, vtd, wos,
                  ident, trimask, xout):
    ctx.sems("dc", "kv0", "kv1", "dxs0", "dxs1", "dxs2", "dxs3",
             "dw0", "dw1", "pe", "act", "dve", "dst0", "dst1")
    T, V, S, G, Y = "tensor", "vector", "scalar", "gpsimd", "sync"

    id_sb = ctx.sb("b_id", [128, 128], BF16)
    tri_sb = ctx.sb("b_tri", [128, 128], BF16)
    qt_sb = ctx.sb("b_qt", [128, 32, TLOC], BF16)
    ot_sb = ctx.sb("b_ot", [128, 32, TLOC], BF16)

    ctx.wait(G, "dc")
    ctx.op(G, lambda g: g.dma_start(out=id_sb[:], in_=ident[:]), "dc", 16)
    ctx.wait(G, "dc")
    ctx.op(G, lambda g: g.dma_start(out=tri_sb[:], in_=trimask[:]), "dc", 16)
    consts = ctx.c["dc"]
    ctx.op(Y, lambda sy: sy.dma_start(out=qt_sb[:], in_=qt[:]), "dst0", 16)
    qt_ready = ("dst0", ctx.c["dst0"])

    ot_ready = [0] * NTB

    with ExitStack() as p1:
        c1 = sub(ctx, p1)
        ka = [c1.sb(f"b_ka{i}", [128, NKA * 128], BF16) for i in range(2)]
        kb_ = [c1.sb(f"b_kb{i}", [128, NKB * 128], BF16) for i in range(2)]
        kd = [c1.sb(f"b_kd{i}", [128, TLOC], BF16) for i in range(2)]
        va = [c1.sb(f"b_va{i}", [128, NKA * 129], BF16) for i in range(2)]
        vb = [c1.sb(f"b_vb{i}", [128, NKB * 129], BF16) for i in range(2)]
        vd = [c1.sb(f"b_vd{i}", [128, 8 * 129], BF16) for i in range(2)]
        probs = [c1.sb(f"b_pr{i}", [128, 512], BF16) for i in range(2)]
        prd = c1.sb("b_prd", [128, 128], BF16)
        rz = c1.sb("b_rz", [128, 1], F32)
        oh = c1.sb("b_oh", [128, 128], BF16)

        pss = [c1.ps(f"b_pss{i}", [128, 512]) for i in range(2)]
        pv = [c1.ps(f"b_pv{i}", [128, 129]) for i in range(4)]
        psd = c1.ps("b_psd", [128, 128])
        psT = c1.ps("b_psT", [128, 128], BF16)

        kv_ready = [None, None]
        kv_free = [0, 0]
        pss_free = [0, 0]
        probs_free = [0, 0]
        pv_free = [0, 0, 0, 0]
        psd_free = 0
        prd_free = 0
        rz_free = 0
        oh_free = 0
        psT_free = 0

        def load_head(h, par):
            for src, dst_t, semn in ((kta, ka, None), (ktb, kb_, None), (ktd, kd, None),
                                     (vta, va, None), (vtb, vb, None), (vtd, vd, None)):
                ctx.wait(Y, "pe", kv_free[par])
                if src in (kta, ktb, ktd):
                    ctx.op(Y, lambda sy, src=src, dst_t=dst_t, h=h, par=par: sy.dma_start(
                        out=dst_t[par][:], in_=src[:, h, :]), f"kv{par}", 16)
                else:
                    ctx.op(Y, lambda sy, src=src, dst_t=dst_t, h=h, par=par: sy.dma_start(
                        out=dst_t[par][:], in_=src[h]), f"kv{par}", 16)
            kv_ready[par] = (f"kv{par}", ctx.c[f"kv{par}"])

        load_head(0, 0)
        for h in range(32):
            par = h % 2
            if h + 1 < 32:
                load_head(h + 1, (h + 1) % 2)
            ctx.wait(T, *kv_ready[par])
            ctx.wait(T, *qt_ready)
            ctx.wait(T, "dc", consts)
            for chunk in range(2):
                nkb = (NKA, NKB)[chunk]
                ksrc = (ka, kb_)[chunk][par]
                vsrc = (va, vb)[chunk][par]
                qc0 = chunk * 512
                for kbi in range(nkb):
                    gp = kbi % 2
                    ctx.wait(T, "act", pss_free[gp])
                    ctx.op(T, lambda t, kbi=kbi, gp=gp, ksrc=ksrc, qc0=qc0, h=h:
                           t.matmul(pss[gp][:],
                                    ksrc[:, kbi * 128:(kbi + 1) * 128],
                                    qt_sb[:, h, qc0:qc0 + 512],
                                    start=True, stop=True), "pe")
                    smm = ctx.c["pe"]
                    ctx.wait(S, "pe", max(smm, probs_free[gp]))
                    e_n = ctx.op(S, lambda s, gp=gp: s.activation(
                        out=probs[gp][:], in_=pss[gp][:], func=AF.Exp), "act")
                    pss_free[gp] = e_n
                    ctx.wait(T, "act", e_n)
                    if kbi == 0:
                        ctx.wait(T, "act", max(pv_free[0], pv_free[1],
                                               pv_free[2], pv_free[3]))
                    for qb in range(4):
                        ctx.op(T, lambda t, kbi=kbi, qb=qb, gp=gp, vsrc=vsrc:
                               t.matmul(pv[qb][:],
                                        probs[gp][:, qb * 128:(qb + 1) * 128],
                                        vsrc[:, kbi * 129:(kbi + 1) * 129],
                                        start=(kbi == 0), stop=False),
                               "pe" if qb == 3 else None)
                    probs_free[gp] = ctx.c["pe"]

                for qb in range(4):
                    lq = chunk * 4 + qb
                    for j in range(qb + 1):
                        lk = chunk * 4 + j
                        ctx.wait(T, "act", psd_free)
                        ctx.op(T, lambda t, lk=lk, lq=lq, h=h, par=par: t.matmul(
                            psd[:], kd[par][:, lk * 128:(lk + 1) * 128],
                            qt_sb[:, h, lq * 128:(lq + 1) * 128],
                            start=True, stop=True), "pe")
                        ctx.wait(S, "pe", max(ctx.c["pe"], prd_free))
                        psd_free = ctx.op(S, lambda s: s.activation(
                            out=prd[:], in_=psd[:], func=AF.Exp), "act")
                        if j == qb:
                            ctx.wait(V, "act", psd_free)
                            m_n = ctx.op(V, lambda v: v.tensor_tensor(
                                out=prd[:], in0=prd[:], in1=tri_sb[:], op=ALU.mult), "dve")
                            ctx.wait(T, "dve", m_n)
                        else:
                            ctx.wait(T, "act", psd_free)
                        ctx.op(T, lambda t, lk=lk, lq=lq, qb=qb, par=par: t.matmul(
                            pv[qb][:], prd[:],
                            vd[par][:, lk * 129:(lk + 1) * 129],
                            start=False, stop=(lk == lq)), "pe")
                        prd_free = ctx.c["pe"]
                pv_mm = ctx.c["pe"]
                for qb in range(4):
                    ctx.wait(V, "pe", pv_mm)
                    ctx.wait(V, "act", rz_free)
                    r_n = ctx.op(V, lambda v, qb=qb: v.reciprocal(
                        rz[:], pv[qb][:, 128:129]), "dve")
                    ctx.wait(S, "dve", r_n)
                    ctx.wait(S, "pe", oh_free)
                    ev_n = ctx.op(S, lambda s, qb=qb: s.activation(
                        out=oh[:], in_=pv[qb][:, 0:128],
                        func=AF.Copy, scale=rz[:]), "act")
                    rz_free = ev_n
                    pv_free[qb] = ev_n
                    ctx.wait(T, "act", ev_n)
                    ctx.wait(T, "dve", psT_free)
                    tp_n = ctx.op(T, lambda t: t.transpose(psT[:], oh[:], id_sb[:]), "pe")
                    oh_free = tp_n
                    ctx.wait(V, "pe", tp_n)
                    psT_free = ctx.op(V, lambda v, h=h, qc0=qc0, qb=qb: v.tensor_copy(
                        ot_sb[:, h, qc0 + qb * 128:qc0 + (qb + 1) * 128], psT[:]), "dve")
            kv_free[par] = ctx.c["pe"]
        p1_pe = ctx.c["pe"]
        p1_dve = ctx.c["dve"]
        p1_act = ctx.c["act"]
        ot_done = ctx.c["dve"]

    with ExitStack() as p2:
        c2 = sub(ctx, p2)
        wsl = [c2.sb(f"b_wsl{i}", [128, 32 * 512], BF16) for i in range(2)]
        xsl = [c2.sb(f"b_xsl{i}", [128, 512], F32) for i in range(4)]
        xo_sb = [c2.sb(f"b_xo{i}", [128, 512], F32) for i in range(2)]
        psp = [c2.ps(f"b_psp{i}", [128, 512]) for i in range(2)]

        wsl_ready = [None, None]
        wsl_free = [0, 0]
        ctx.wait(G, "pe", p1_pe)
        ctx.wait(G, "dve", p1_dve)
        ctx.wait(G, "act", p1_act)
        ctx.wait(T, "act", p1_act)
        ctx.op(G, lambda g: g.dma_start(out=wsl[0][:], in_=wos[0]), "dw0", 16)
        wsl_ready[0] = ("dw0", ctx.c["dw0"])

        psp_free = [0, 0]
        xsl_free = [0] * 4
        xo_free = [0, 0]
        it = 0
        for oc in range(8):
            wp = oc % 2
            if oc + 1 < 8:
                npar = (oc + 1) % 2
                ctx.wait(G, "pe", wsl_free[npar])
                ctx.op(G, lambda g, oc=oc, npar=npar: g.dma_start(
                    out=wsl[npar][:], in_=wos[oc + 1]), f"dw{npar}", 16)
                wsl_ready[npar] = (f"dw{npar}", ctx.c[f"dw{npar}"])
            for tb in range(NTB):
                par = it % 2
                xp = it % 4
                ctx.wait(Y, "dve", xsl_free[xp])
                ctx.op(Y, lambda sy, tb=tb, oc=oc, xp=xp: sy.dma_start(
                    out=xsl[xp][:], in_=xin[tb * 128:(tb + 1) * 128, oc * 512:(oc + 1) * 512]),
                    f"dxs{xp}", 16)
                xsl_n = (f"dxs{xp}", ctx.c[f"dxs{xp}"])

                ctx.wait(T, *wsl_ready[wp])
                ctx.wait(T, "dve", max(ot_done, psp_free[par]))
                for k in range(32):
                    mm_n = ctx.op(T, lambda t, tb=tb, k=k, par=par, wp=wp: t.matmul(
                        psp[par][:], ot_sb[:, k, tb * 128:(tb + 1) * 128],
                        wsl[wp][:, k * 512:(k + 1) * 512],
                        start=(k == 0), stop=(k == 31)), "pe" if k == 31 else None)
                if tb == NTB - 1:
                    wsl_free[wp] = mm_n
                ctx.wait(V, "pe", mm_n)
                ctx.wait(V, *xsl_n)
                ctx.wait(V, f"dst{par}", xo_free[par])
                ep_n = ctx.op(V, lambda v, par=par, xp=xp: v.tensor_tensor(
                    out=xo_sb[par][:], in0=psp[par][:], in1=xsl[xp][:],
                    op=ALU.add), "dve")
                psp_free[par] = ep_n
                xsl_free[xp] = ep_n
                ctx.wait(Y, "dve", ep_n)
                xo_free[par] = ctx.op(Y, lambda sy, tb=tb, oc=oc, par=par: sy.dma_start(
                    out=xout[tb * 128:(tb + 1) * 128, oc * 512:(oc + 1) * 512],
                    in_=xo_sb[par][:]), f"dst{par}", 16)
                it += 1


def stage_l1attn_b(inp, s5a_outs):
    import ml_dtypes
    bf = ml_dtypes.bfloat16
    kt0 = np.asarray(s5a_outs[0]["kt"])
    fast = kt0.dtype.itemsize == 2
    if fast:
        conv = lambda a: np.asarray(a).view(np.uint16)
        out = lambda a: np.ascontiguousarray(a).view(bf)
        wt = np.uint16
        ONE = np.float32(1.0).astype(bf).view(np.uint16)
    else:
        conv = lambda a: np.asarray(a)
        out = lambda a: np.ascontiguousarray(a)
        wt = kt0.dtype
        ONE = wt.type(1.0) if hasattr(wt, "type") else 1.0
    wo = np.asarray(inp["L1_attn_wo"], np.float32)
    k_idx, q_idx = np.meshgrid(np.arange(128), np.arange(128), indexing="ij")
    shared = {
        "wo_s": pack_slabs(np.ascontiguousarray(wo.T), 512),
        "ident": np.eye(128, dtype=np.float32),
        "trimask": (k_idx <= q_idx).astype(np.float32),
    }
    percore = [None] * 8
    for bb in range(2):
        cores = [bb * 4 + c for c in range(4)]
        ktg = np.zeros((128, 32, 4096), wt)
        vg = np.zeros((4096, 32, 129), wt)
        for c in range(4):
            co = cores[c]
            kt_c = conv(s5a_outs[co]["kt"])
            vt_c = conv(s5a_outs[co]["vt"]).reshape(1024, 32, 128)
            for half, u in ((0, c), (1, 7 - c)):
                ktg[:, :, u * 512:(u + 1) * 512] = kt_c[:, :, half * 512:(half + 1) * 512]
                vg[u * 512:(u + 1) * 512, :, :128] = vt_c[half * 512:(half + 1) * 512]
                vg[u * 512:(u + 1) * 512, :, 128] = ONE
        for c in range(4):
            co = cores[c]
            kta = ktg[:, :, :NKA * 128].copy()
            kta[:, :, 512 * c:] = 0
            ktb = ktg[:, :, :NKB * 128].copy()
            ktb[:, :, 512 * (7 - c):] = 0
            va = vg[:NKA * 128].copy()
            va[512 * c:] = 0
            vb = vg[:NKB * 128].copy()
            vb[512 * (7 - c):] = 0
            vd = np.zeros((1024, 32, 129), wt)
            vd[:, :, :128] = conv(s5a_outs[co]["vt"]).reshape(1024, 32, 128)
            vd[:, :, 128] = ONE

            def vhead(vv, nkb):
                a = vv.reshape(nkb, 128, 32, 129).transpose(2, 1, 0, 3)
                return out(a.reshape(32, 128, nkb * 129))

            percore[co] = {
                "qt": out(conv(s5a_outs[co]["qt"])),
                "kta": out(kta),
                "ktb": out(ktb),
                "ktd": out(conv(s5a_outs[co]["kt"])),
                "vta": vhead(va, NKA),
                "vtb": vhead(vb, NKB),
                "vtd": vhead(vd, 8),
            }
    return shared, percore


_PROGS = {}


def _prog(name, builder):
    if name not in _PROGS:
        _PROGS[name] = builder()
    return _PROGS[name]


def _run(nc, in_maps):
    from concourse.bass_utils import run_bass_kernel_spmd
    return run_bass_kernel_spmd(nc, in_maps, core_ids=list(range(NCORES)))


def _gather(res, like):
    sh = [res.results[c]["xout"] for c in range(NCORES)]
    out = np.empty(like.shape, sh[0].dtype)
    unslice_owned(sh, out)
    return out


def kernel(**inputs):
    x = np.ascontiguousarray(np.asarray(inputs["L0_attn_inputs"], np.float32))

    nc1 = _prog("l0ffn", build_l0ffn_program)
    w1 = stage_l0ffn_weights(inputs)
    res = _run(nc1, [dict(w1, xin=slice_owned(x, c)) for c in range(NCORES)])
    x1 = _gather(res, x)

    nc2 = _prog("l0attn", build_l0attn_program)
    w2 = stage_l0attn_weights(inputs)
    res = _run(nc2, [dict(w2, xin_h=slice_haloed(x1, c)) for c in range(NCORES)])
    x2 = _gather(res, x)

    res = _run(nc1, [dict(w1, xin=slice_owned(x2, c)) for c in range(NCORES)])
    x3 = _gather(res, x)

    nc4 = _prog("l1ffn", build_l1ffn_program)
    w4 = stage_l1ffn_weights(inputs)
    res = _run(nc4, [dict(w4, xin=slice_owned(x3, c)) for c in range(NCORES)])
    x4 = _gather(res, x)

    nc5a = _prog("l1attn_a", build_l1attn_a_program)
    w5a = stage_l1attn_a_weights(inputs)
    res = _run(nc5a, [dict(w5a, xin=slice_owned(x4, c), cs_cols=cs_cols_for(c))
                      for c in range(NCORES)])
    s5a_outs = [res.results[c] for c in range(NCORES)]
    nc5b = _prog("l1attn_b", build_l1attn_b_program)
    shared, percore = stage_l1attn_b(inputs, s5a_outs)
    res = _run(nc5b, [dict(shared, **percore[c], xin=slice_owned(x4, c))
                      for c in range(NCORES)])
    x5 = _gather(res, x)

    res = _run(nc4, [dict(w4, xin=slice_owned(x5, c)) for c in range(NCORES)])
    x6 = _gather(res, x)
    return x6
